# version 26
# baseline (speedup 1.0000x reference)
import os
import numpy as np
import ml_dtypes
import concourse.bass as bass
import concourse.mybir as mybir
from concourse.bass_utils import run_bass_kernel_spmd

F32 = mybir.dt.float32
BF16 = mybir.dt.bfloat16
I8 = mybir.dt.int8
AF = mybir.ActivationFunctionType
ALU = mybir.AluOpType
AX = mybir.AxisListType

ENGS = ("tensor", "scalar", "vector", "gpsimd", "sync")
NEG = -1.0e30


class Cfg:
    def __init__(self, B=16, S=2048, H=2048, NH=16, I=8192, n_cores=8):
        self.B, self.S, self.H, self.NH, self.I = B, S, H, NH, I
        self.n_cores = n_cores
        self.BPC = B // n_cores
        self.T = self.BPC * S
        self.HK = H // 128
        self.IK = I // 128
        self.SK = S // 128
        self.FK = 3 * NH
        self.TB = 512
        self.NTB = self.T // self.TB
        self.NG = S // 512
        assert S % 512 == 0 and H % 128 == 0 and I % 128 == 0


class Sem:
    def __init__(self, h):
        self.h = h
        self.n = 0


class PB:

    def __init__(self):
        self.q = {e: [] for e in ENGS}
        self.mark = {e: {} for e in ENGS}
        self.pending_barrier = {e: None for e in ENGS}
        self.dma_reg = []

    def op(self, eng, fn, waits=(), inc=None, k=1):
        if self.pending_barrier[eng] is not None:
            waits = list(waits) + self.pending_barrier[eng]
            self.pending_barrier[eng] = None
        wd = {}
        for w in waits:
            if w is None:
                continue
            s, v = w
            if v and (s not in wd or v > wd[s]):
                wd[s] = v
        m = self.mark[eng]
        todo = []
        for s, v in wd.items():
            if m.get(s, 0) < v:
                todo.append((s.h, v))
                m[s] = v
        todo = tuple(todo)
        tick = None
        if inc is not None:
            inc.n += k
            tick = (inc, inc.n)

        def run(e, _w=todo, _fn=fn, _inc=inc, _k=k):
            for (h, v) in _w:
                e.wait_ge(h, v)
            ins = _fn(e)
            if _inc is not None:
                ins.then_inc(_inc.h, _k)

        self.q[eng].append(run)
        return tick

    def barrier(self, csems, fence_sems):
        sb_s, sb_g = fence_sems
        t_s = self.op("sync", lambda e: e.nop(),
                      waits=[(s, s.n) for (s, q) in self.dma_reg
                             if q == "sync" and s.n > 0], inc=sb_s)
        t_g = self.op("gpsimd", lambda e: e.nop(),
                      waits=[(s, s.n) for (s, q) in self.dma_reg
                             if q == "gpsimd" and s.n > 0], inc=sb_g)
        snap = [t_s, t_g] + [(s, s.n) for s in csems if s.n > 0]
        for e in ENGS:
            self.pending_barrier[e] = list(snap)


def host_prep(cfg, inputs):
    c = cfg
    g = {}
    xw = np.ascontiguousarray(np.asarray(inputs["L0_attn_qkv_w"], dtype=np.float32))
    g["wqkv_t"] = np.ascontiguousarray(
        xw.reshape(c.HK, 128, c.FK, 128).transpose(2, 1, 0, 3))
    ow = np.ascontiguousarray(np.asarray(inputs["L0_attn_out_w"], dtype=np.float32))
    g["wout_t"] = np.ascontiguousarray(
        ow.reshape(c.NH, 128, c.HK, 128).transpose(2, 1, 0, 3))
    wg = np.asarray(inputs["L0_ffn_w_gate"])
    wu = np.asarray(inputs["L0_ffn_w_up"])
    wd = np.asarray(inputs["L0_ffn_w_down"])
    g["wg_t"] = np.ascontiguousarray(
        wg.reshape(c.HK, 128, c.IK, 128).transpose(2, 1, 0, 3))
    g["wu_t"] = np.ascontiguousarray(
        wu.reshape(c.HK, 128, c.IK, 128).transpose(2, 1, 0, 3))
    g["wd_t"] = np.ascontiguousarray(
        wd.reshape(c.IK, 128, c.HK, 128).transpose(2, 1, 0, 3))
    g["qkvb_c"] = np.ascontiguousarray(
        np.asarray(inputs["L0_attn_qkv_b"], dtype=np.float32).reshape(c.FK, 128).T)
    g["outb_c"] = np.ascontiguousarray(
        np.asarray(inputs["L0_attn_out_b"], dtype=np.float32).reshape(c.HK, 128).T)
    g["sg_c"] = np.ascontiguousarray(
        np.asarray(inputs["L0_ffn_s_gate"], dtype=np.float32).reshape(c.IK, 128).T)
    g["su_c"] = np.ascontiguousarray(
        np.asarray(inputs["L0_ffn_s_up"], dtype=np.float32).reshape(c.IK, 128).T)
    g["sd_c"] = np.ascontiguousarray(
        np.asarray(inputs["L0_ffn_s_down"], dtype=np.float32).reshape(c.HK, 128).T)
    g["ident_bf"] = np.eye(128, dtype=ml_dtypes.bfloat16)
    g["ident_f"] = np.eye(128, dtype=np.float32)
    g["ones_col_bf"] = np.ones((128, 1), dtype=ml_dtypes.bfloat16)
    g["ones_row_bf"] = np.ones((1, 128), dtype=ml_dtypes.bfloat16)
    g["ones_row_f"] = np.ones((1, 128), dtype=np.float32)
    jj = np.arange(128)[:, None]
    ii = np.arange(128)[None, :]
    g["tri_t"] = np.where(jj > ii, np.float32(NEG), np.float32(0.0)).astype(np.float32)
    if c.NH <= 8:
        slopes = np.array([1.0 / 2.0 ** i for i in range(c.NH)], dtype=np.float32)
    else:
        slopes = np.array([1.0 / 2.0 ** (i / 2.0) for i in range(c.NH)],
                          dtype=np.float32)
    al = np.zeros((128, c.NH * c.SK), dtype=np.float32)
    for h in range(c.NH):
        for jt in range(c.SK):
            al[:, h * c.SK + jt] = slopes[h] * (jt * 128 + np.arange(128))
    g["alibi_c"] = al
    arow = np.zeros((c.NH, c.S), dtype=np.float64)
    for h in range(c.NH):
        arow[h] = slopes[h].astype(np.float64) * np.arange(c.S)
    hi = arow.astype(ml_dtypes.bfloat16)
    lo = (arow - hi.astype(np.float64)).astype(ml_dtypes.bfloat16)
    g["arow_hi"] = np.ascontiguousarray(hi)
    g["arow_lo"] = np.ascontiguousarray(lo)
    g["tri_n"] = np.where(jj < ii, np.float32(NEG),
                          np.float32(0.0)).astype(np.float32)
    g["eps_c"] = np.full((1, 1), 1e-5, dtype=np.float32)
    x = np.asarray(inputs["L0_attn_x"], dtype=np.float32)
    xs = [np.ascontiguousarray(
        x[cid * c.BPC:(cid + 1) * c.BPC].reshape(c.T, c.H))
        for cid in range(c.n_cores)]
    return g, xs


def build_program(cfg, debug_outputs=False, use_arena=True):
    c = cfg
    nc = bass.Bass("TRN2", target_bir_lowering=False)
    pb = PB()

    di = {}
    def dram_in(name, shape, dt_):
        di[name] = nc.dram_tensor(name, list(shape), dt_, kind="ExternalInput")
    dram_in("x_nat", (c.T, c.H), F32)
    dram_in("wqkv_t", (c.FK, 128, c.HK, 128), F32)
    dram_in("wout_t", (c.HK, 128, c.NH, 128), F32)
    dram_in("wg_t", (c.IK, 128, c.HK, 128), I8)
    dram_in("wu_t", (c.IK, 128, c.HK, 128), I8)
    dram_in("wd_t", (c.HK, 128, c.IK, 128), I8)
    dram_in("qkvb_c", (128, c.FK), F32)
    dram_in("outb_c", (128, c.HK), F32)
    dram_in("sg_c", (128, c.IK), F32)
    dram_in("su_c", (128, c.IK), F32)
    dram_in("sd_c", (128, c.HK), F32)
    dram_in("ident_bf", (128, 128), BF16)
    dram_in("ident_f", (128, 128), F32)
    dram_in("ones_col_bf", (128, 1), BF16)
    dram_in("ones_row_bf", (1, 128), BF16)
    dram_in("ones_row_f", (1, 128), F32)
    dram_in("tri_t", (128, 128), F32)
    dram_in("alibi_c", (128, c.NH * c.SK), F32)
    dram_in("arow_hi", (c.NH, c.S), BF16)
    dram_in("arow_lo", (c.NH, c.S), BF16)
    dram_in("tri_n", (128, 128), F32)
    dram_in("eps_c", (1, 1), F32)

    dbg = "ExternalOutput" if debug_outputs else "Internal"
    out_nat = nc.dram_tensor("out_nat", [c.T, c.H], F32, kind="ExternalOutput")
    xT_a = nc.dram_tensor("xT_a", [c.HK, 128, c.T], F32, kind=dbg)
    xT_b = nc.dram_tensor("xT_b", [c.HK, 128, c.T], F32, kind=dbg)
    ctxT_d = nc.dram_tensor("ctxT_d", [c.NH, 128, c.S], BF16, kind=dbg)
    qkvT_d = nc.dram_tensor("qkvT_d", [c.BPC * c.NH, 3, 128, c.S], BF16,
                            kind="Internal")
    bounce_d = nc.dram_tensor("bounce_d", [1, 2 * c.S], BF16, kind="Internal")

    import contextlib
    es = contextlib.ExitStack()
    nc._keepalive = es
    STn = Sem(es.enter_context(nc.semaphore("STn")))
    SS = Sem(es.enter_context(nc.semaphore("SS")))
    SV = Sem(es.enter_context(nc.semaphore("SV")))
    SB_s = Sem(es.enter_context(nc.semaphore("SBs")))
    SB_g = Sem(es.enter_context(nc.semaphore("SBg")))
    CSEMS = (STn, SS, SV, SB_s, SB_g)

    def mkds(name, queue):
        sm = Sem(es.enter_context(nc.semaphore(name)))
        pb.dma_reg.append((sm, queue))
        return sm

    def barrier():
        pb.barrier(CSEMS, (SB_s, SB_g))

    def sb(name, shape, dt_):
        return es.enter_context(nc.sbuf_tensor(name, list(shape), dt_))

    class _FakeArena:
        def sbuf_tensor(self, name, shape, dt_):
            return sb(name, shape, dt_)

        def allocate(self):
            pass

        def free(self):
            pass

    def mk_arena():
        return nc.auto_arena() if use_arena else _FakeArena()

    cst = {}
    for n, shape, dt_ in (
        ("ident_bf", (128, 128), BF16), ("ident_f", (128, 128), F32),
        ("ones_col_bf", (128, 1), BF16), ("ones_row_bf", (1, 128), BF16),
        ("ones_row_f", (1, 128), F32), ("tri_t", (128, 128), F32),
        ("alibi_c", (128, c.NH * c.SK), F32), ("qkvb_c", (128, c.FK), F32),
        ("outb_c", (128, c.HK), F32), ("sg_c", (128, c.IK), F32),
        ("su_c", (128, c.IK), F32), ("sd_c", (128, c.HK), F32),
        ("tri_n", (128, 128), F32), ("eps_c", (1, 1), F32),
    ):
        cst[n] = sb("c_" + n, shape, dt_)
    half_sd = sb("half_sd", (128, c.HK), F32)

    ps = es.enter_context(nc.psum_tensor("ps", [128, 3584], F32))
    ps_bf = es.enter_context(nc.psum_tensor("ps_bf", [128, 1024], BF16))

    def bank(i, w=512):
        return ps[:, i * 512:i * 512 + w]

    s_cst = mkds("s_cst", "sync")
    t_cst = []
    for n in cst:
        t_cst.append(pb.op("sync", lambda e, a=cst[n], b=di[n]:
                           e.dma_start(a[:], b[:]), inc=s_cst, k=16))
    pb.op("vector", lambda e: e.tensor_scalar_mul(
        half_sd[:], cst["sd_c"][:], 0.5), waits=t_cst, inc=SV)
    barrier()

    ar0 = mk_arena()
    xnat_b = [ar0.sbuf_tensor(f"p0_xn{i}", [128, c.H], F32) for i in range(2)]
    xts_b = ar0.sbuf_tensor("p0_xts", [128, c.HK, 512], F32)
    ar0.allocate()
    s_xn = [mkds(f"s_xn{i}", "sync") for i in range(2)]
    s_xts = mkds("s_xts", "sync")
    t_evac_hist = []
    t_store = []
    for g in range(c.NTB):
        evac_this = []
        for tt in range(4):
            gt = g * 4 + tt
            par = gt % 2
            t_xn = pb.op(
                "sync", lambda e, p=par, r=gt: e.dma_start(
                    xnat_b[p][:], di["x_nat"][r * 128:(r + 1) * 128, :]),
                waits=t_evac_hist[-2 * c.HK:], inc=s_xn[par], k=16)
            for hk in range(c.HK):
                pbk = (tt * c.HK + hk) % 4
                wt = [t_xn]
                if len(t_evac_hist) >= 4:
                    wt.append(t_evac_hist[-4])
                t_tr = pb.op("tensor", lambda e, p=par, h=hk, b=pbk: e.transpose(
                    bank(b, 128), xnat_b[p][:, h * 128:(h + 1) * 128],
                    cst["ident_f"][:]), waits=wt, inc=STn)
                t_ev = pb.op("scalar", lambda e, h=hk, t=tt, b=pbk: e.copy(
                    xts_b[:, h, t * 128:(t + 1) * 128], bank(b, 128)),
                    waits=[t_tr] + (t_store[-1:] if t_store else []), inc=SS)
                t_evac_hist.append(t_ev)
                evac_this.append(t_ev)
        for q4 in range(0, c.HK, 4):
            nh = min(4, c.HK - q4)
            t_store.append(pb.op(
                "sync", lambda e, g_=g, q=q4, n=nh: e.dma_start(
                    xT_a.rearrange("k p t -> p k t")[:, q:q + n,
                                                     g_ * 512:(g_ + 1) * 512],
                    xts_b[:, q:q + n, :]),
                waits=evac_this, inc=s_xts, k=16))
    barrier()
    ar0.free()

    ln_xt = [sb(f"ln_xt{i}", (128, 512), F32) for i in range(4)]
    s_ln = [mkds(f"s_ln{i}", "sync") for i in range(4)]
    ln_xbf = [sb(f"ln_xbf{i}", (128, 512), BF16) for i in range(2)]
    ln_sq = [sb(f"ln_sq{i}", (128, 512), BF16) for i in range(2)]
    r_m = sb("r_m", (1, 512), F32)
    r_msq = sb("r_msq", (1, 512), F32)
    r_var = sb("r_var", (1, 512), F32)
    r_r = sb("r_r", (1, 512), F32)
    r_nmr = r_msq
    r_sd = r_var
    R_bc = sb("R_bc", (128, 512), F32)
    NMR_bc = sb("NMR_bc", (128, 512), F32)
    ln_state = {"apply": [], "b6_reader": {}, "rowmm": [],
                "slot_users": {}}

    def emit_ln(src, tok0, dst_slice_fn, dst_free=(), lbank=6, split=False):
        HK = c.HK
        sx = ps[0:1, lbank * 512:lbank * 512 + 512]
        sq = ps[32:33, lbank * 512:lbank * 512 + 512]
        t_mm_x = t_mm_q = None
        t_cast = {}
        b6w = [ln_state["b6_reader"].get(lbank)]
        for hk in range(HK):
            tp = hk % 4
            bp = hk % 2
            t_ld = pb.op("sync", lambda e, h=hk, p=tp, t0=tok0: e.dma_start(
                ln_xt[p][:], src[h, :, t0:t0 + 512]),
                waits=t_cast.get(hk - 4, []) +
                [ln_state["slot_users"].get(tp)], inc=s_ln[tp], k=16)
            t_c1 = pb.op("scalar", lambda e, p=tp, b=bp: e.copy(
                ln_xbf[b][:], ln_xt[p][:]),
                waits=[t_ld] + ([t_mm_x] if hk >= 2 else []), inc=SS)
            t_c2 = pb.op("scalar", lambda e, p=tp, b=bp: e.activation(
                ln_sq[b][:], ln_xt[p][:], AF.Square),
                waits=[t_ld] + ([t_mm_q] if hk >= 2 else []), inc=SS)
            t_cast[hk] = [t_c1, t_c2]
            t_mm_x = pb.op("tensor", lambda e, b=bp, h=hk: e.matmul(
                sx, cst["ones_col_bf"][:], ln_xbf[b][:],
                start=(h == 0), stop=(h == HK - 1)),
                waits=([t_c1] + b6w) if hk == 0 else [t_c1], inc=STn)
            t_mm_q = pb.op("tensor", lambda e, b=bp, h=hk: e.matmul(
                sq, cst["ones_col_bf"][:], ln_sq[b][:],
                start=(h == 0), stop=(h == HK - 1)), waits=[t_c2], inc=STn)
        inv = 1.0 / c.H
        rw = ln_state["rowmm"]
        t1 = pb.op("vector", lambda e: e.tensor_scalar_mul(r_m[:], sx, inv),
                   waits=[t_mm_x, t_mm_q] + rw, inc=SV)
        t2 = pb.op("vector", lambda e: e.tensor_scalar_mul(r_msq[:], sq, inv),
                   waits=[t_mm_q], inc=SV)
        t3 = pb.op("vector", lambda e: e.tensor_mul(r_var[:], r_m[:], r_m[:]),
                   waits=[t1], inc=SV)
        t4 = pb.op("vector", lambda e: e.tensor_sub(r_var[:], r_msq[:], r_var[:]),
                   waits=[t2, t3], inc=SV)
        t5 = pb.op("scalar", lambda e: e.activation(
            r_sd[:], r_var[:], AF.Sqrt, bias=cst["eps_c"][:]), waits=[t4], inc=SS)
        t6 = pb.op("vector", lambda e: e.reciprocal(r_r[:], r_sd[:]),
                   waits=[t5], inc=SV)
        t7 = pb.op("vector", lambda e: e.tensor_mul(r_nmr[:], r_m[:], r_r[:]),
                   waits=[t6], inc=SV)
        t8 = pb.op("vector", lambda e: e.tensor_scalar_mul(r_nmr[:], r_nmr[:], -1.0),
                   waits=[t7], inc=SV)
        def finish(tmx=t_mm_x, tmq=t_mm_q, tcst=t_cast):
            tb1 = pb.op("tensor", lambda e, lb=lbank: e.matmul(
                bank(lb), cst["ones_row_f"][:], r_r[:], start=True, stop=True),
                waits=[t6, t1, t2], inc=STn)
            tc1 = pb.op("vector", lambda e, lb=lbank: e.tensor_copy(
                R_bc[:], bank(lb)), waits=[tb1] + ln_state["apply"], inc=SV)
            tb2 = pb.op("tensor", lambda e, lb=lbank: e.matmul(
                bank(lb), cst["ones_row_f"][:], r_nmr[:], start=True, stop=True),
                waits=[t8, tc1], inc=STn)
            tc2 = pb.op("vector", lambda e, lb=lbank: e.tensor_copy(
                NMR_bc[:], bank(lb)), waits=[tb2], inc=SV)
            ln_state["b6_reader"][lbank] = tc2
            ln_state["rowmm"] = [tb1, tb2]
            out_ticks = []
            applies = []
            for hk in range(HK):
                tp = hk % 4
                t_ld = pb.op("sync", lambda e, h=hk, p=tp, t0=tok0: e.dma_start(
                    ln_xt[p][:], src[h, :, t0:t0 + 512]),
                    waits=(tcst.get(hk, []) +
                           ([tmx, tmq] if hk < 4 else []) +
                           (applies[-4:] if len(applies) >= 4 else [])),
                    inc=s_ln[tp], k=16)
                t_a1 = pb.op("vector", lambda e, p=tp: e.tensor_mul(
                    ln_xt[p][:], ln_xt[p][:], R_bc[:]), waits=[t_ld, tc1], inc=SV)
                t_a2 = pb.op("vector", lambda e, p=tp, h=hk: e.tensor_add(
                    dst_slice_fn(h), ln_xt[p][:], NMR_bc[:]),
                    waits=[t_a1, tc2] + list(dst_free), inc=SV)
                applies.append(t_a2)
                out_ticks.append(t_a2)
                ln_state["slot_users"][tp] = t_a2
            ln_state["apply"] = applies[-2:]
            return out_ticks, tc2
        if split:
            return finish
        return finish()

    def emit_ffn(src, dst, final_pass):
        tag = "f2" if final_pass else "f1"
        ar = mk_arena()
        xln = [ar.sbuf_tensor(f"{tag}_xln{i}", [128, c.HK, 512], BF16)
               for i in range(2)]
        fuse = ar.sbuf_tensor(f"{tag}_fuse", [128, c.IK, 512], BF16)
        wgb = [ar.sbuf_tensor(f"{tag}_wg{i}", [128, c.HK, 128], BF16)
               for i in range(4)]
        wub = [ar.sbuf_tensor(f"{tag}_wu{i}", [128, c.HK, 128], BF16)
               for i in range(4)]
        wdb = [ar.sbuf_tensor(f"{tag}_wd{i}", [128, c.IK, 128], BF16)
               for i in range(2)]
        gat = [ar.sbuf_tensor(f"{tag}_gat{i}", [128, 512], F32)
               for i in range(2)]
        rxt = [ar.sbuf_tensor(f"{tag}_rxt{i}", [128, 512], F32)
               for i in range(4)]
        xnew = [ar.sbuf_tensor(f"{tag}_xnw{i}", [128, 512], F32)
               for i in range(2)]
        stg = [ar.sbuf_tensor(f"{tag}_stg{i}", [128, 128], F32)
               for i in range(4)] if final_pass else None
        ar.allocate()
        s_wg = [mkds(f"{tag}_swg{i}", "gpsimd") for i in range(4)]
        s_wu = [mkds(f"{tag}_swu{i}", "gpsimd") for i in range(4)]
        s_wd = [mkds(f"{tag}_swd{i}", "gpsimd") for i in range(2)]
        s_rx = [mkds(f"{tag}_srx{i}", "sync") for i in range(4)]
        s_xst = [mkds(f"{tag}_sxst{i}", "sync") for i in range(2)]
        s_on = [mkds(f"{tag}_son{i}", "sync") for i in range(4)]

        prev_up_mm = {}
        last_blk_mms = []
        t_fuse = {}
        t_sil = {}
        t_dn_mm = {}
        t_dn_ev = {}
        t_xn_st = {}
        t_tr_ev = []
        t_trs = []
        t_onat_st = [None] * 4
        xnw_tr_user = {}
        pend_tr = [None]
        prev_block_dn = []
        rx_user = {}
        pf = {}
        PAIRS = ((0, 2), (1, 3), (4, 5))
        pair_sil = {0: None, 1: None, 2: None}
        pair_fus = {0: None, 1: None, 2: None}
        t_xln_next, _ = emit_ln(src, 0, lambda h: xln[0][:, h, :],
                                dst_free=[])
        for blk in range(c.NTB):
            tok0 = blk * c.TB
            xp = blk % 2
            t_xln = t_xln_next
            gate_w0_extra = t_tr_ev[-4:] if final_pass else []
            prev_up_mm = {k - c.IK: prev_up_mm.get(k)
                          for k in range(c.IK - 4, c.IK)}
            t_sil = {-2: t_sil.get(c.IK - 2), -1: t_sil.get(c.IK - 1)}
            t_fuse = {-2: t_fuse.get(c.IK - 2), -1: t_fuse.get(c.IK - 1)}
            for it in range(c.IK):
                par = it % 2
                sl4 = (blk * c.IK + it) % 4
                if (blk, it) in pf:
                    t_wg1, t_wg2 = pf.pop((blk, it))
                else:
                    w_w = [prev_up_mm.get(it - 4)]
                    t_wg1 = pb.op("gpsimd", lambda e, i=it, p=sl4: e.dma_start(
                        wgb[p][:], di["wg_t"][i]), waits=w_w,
                        inc=s_wg[sl4], k=16)
                    t_wg2 = pb.op("gpsimd", lambda e, i=it, p=sl4: e.dma_start(
                        wub[p][:], di["wu_t"][i]), waits=w_w,
                        inc=s_wu[sl4], k=16)
                pi = (blk * c.IK + it) % 3
                gb, ub = PAIRS[pi]
                tg = tu = None
                wg0 = [t_wg1] + t_xln + gate_w0_extra + [pair_sil[pi]]
                for hk in range(c.HK):
                    tg = pb.op("tensor", lambda e, p=sl4, h=hk, b=gb, x=xp:
                               e.matmul(bank(b), wgb[p][:, h, :], xln[x][:, h, :],
                                        start=(h == 0), stop=(h == c.HK - 1)),
                               waits=wg0 if hk == 0 else (), inc=STn)
                wu0 = [t_wg2] + [pair_fus[pi]]
                for hk in range(c.HK):
                    tu = pb.op("tensor", lambda e, p=sl4, h=hk, b=ub, x=xp:
                               e.matmul(bank(b), wub[p][:, h, :], xln[x][:, h, :],
                                        start=(h == 0), stop=(h == c.HK - 1)),
                               waits=wu0 if hk == 0 else (), inc=STn)
                prev_up_mm[it] = tu
                ts = pb.op("scalar", lambda e, i=it, b=gb, p=par: e.activation(
                    gat[p][:], bank(b), AF.Silu, scale=cst["sg_c"][:, i:i + 1]),
                    waits=[tg, t_sil.get(it - 1), t_fuse.get(it - 2)],
                    inc=SS)
                t_sil[it] = ts
                pair_sil[pi] = ts
                t_fuse[it] = pb.op(
                    "vector", lambda e, i=it, b=ub, p=par: e.scalar_tensor_tensor(
                        fuse[:, i, :], bank(b), cst["su_c"][:, i:i + 1],
                        gat[p][:], ALU.mult, ALU.mult),
                    waits=[tu, ts] + prev_block_dn, inc=SV)
                pair_fus[pi] = t_fuse[it]
                if it == min(24, c.IK - 2) and blk + 1 < c.NTB:
                    ln_fin = emit_ln(
                        src, (blk + 1) * c.TB,
                        lambda h, x=(blk + 1) % 2: xln[x][:, h, :],
                        dst_free=[prev_up_mm.get(it - 2)], split=True)
                if it == min(40, c.IK - 1) and blk + 1 < c.NTB:
                    t_xln_next, _ = ln_fin()
            last_blk_mms = [prev_up_mm[c.IK - 1], prev_up_mm[c.IK - 2]]
            all_fuse = [t_fuse[i] for i in range(c.IK)]
            prev_block_dn = []
            t_dn_mm = {-2: t_dn_mm.get(c.HK - 2), -1: t_dn_mm.get(c.HK - 1)}
            t_dn_ev = {k - c.HK: t_dn_ev.get(c.HK + (k - c.HK))
                       for k in range(c.HK - 4, c.HK)}
            t_rx_all = {}
            for ht in range(min(4, c.HK)):
                t_rx_all[ht] = pb.op(
                    "sync", lambda e, h=ht, p=ht % 4, t0=tok0: e.dma_start(
                        rxt[p][:], src[h, :, t0:t0 + 512]),
                    waits=[rx_user.get(ht % 4)], inc=s_rx[ht % 4], k=16)
            for ht in range(c.HK):
                par = ht % 2
                if ht == min(2, c.HK - 1) and blk + 1 < c.NTB:
                    for nit in range(min(3, c.IK)):
                        sl4 = ((blk + 1) * c.IK + nit) % 4
                        w_w = [prev_up_mm.get(c.IK - 4 + nit)]
                        t1 = pb.op("gpsimd", lambda e, i=nit, p=sl4:
                                   e.dma_start(wgb[p][:], di["wg_t"][i]),
                                   waits=w_w, inc=s_wg[sl4], k=16)
                        t2 = pb.op("gpsimd", lambda e, i=nit, p=sl4:
                                   e.dma_start(wub[p][:], di["wu_t"][i]),
                                   waits=w_w, inc=s_wu[sl4], k=16)
                        pf[(blk + 1, nit)] = (t1, t2)
                t_wd = pb.op("gpsimd", lambda e, h=ht, p=par: e.dma_start(
                    wdb[p][:], di["wd_t"][h]),
                    waits=[t_dn_mm.get(ht - 2)], inc=s_wd[par], k=16)
                t_rx = t_rx_all[ht]
                rp4 = ht % 4
                dmm = None
                w0 = [t_wd] + all_fuse + [t_dn_ev.get(ht - 2)]
                for it in range(c.IK):
                    dmm = pb.op("tensor", lambda e, i=it, h=ht, p=par:
                                e.matmul(bank(4 + p), wdb[p][:, i, :],
                                         fuse[:, i, :], start=(i == 0),
                                         stop=(i == c.IK - 1)),
                                waits=w0 if it == 0 else (), inc=STn)
                t_dn_mm[ht] = dmm
                if final_pass and pend_tr[0] is not None \
                        and pend_tr[0][1] == par:
                    emit_tr(*pend_tr[0])
                    pend_tr[0] = None
                wev = [dmm, t_rx]
                if final_pass:
                    wev += [xnw_tr_user.get(par)]
                else:
                    wev += [t_xn_st.get(par)]
                t_ev = pb.op("vector", lambda e, h=ht, p=par, r4=rp4:
                             e.scalar_tensor_tensor(
                                 xnew[p][:], bank(4 + p), half_sd[:, h:h + 1],
                                 rxt[r4][:], ALU.mult, ALU.add),
                             waits=wev, inc=SV)
                t_dn_ev[ht] = t_ev
                rx_user[rp4] = t_ev
                prev_block_dn.append(t_ev)
                if ht + 4 < c.HK:
                    t_rx_all[ht + 4] = pb.op(
                        "sync", lambda e, h=ht + 4, p=(ht + 4) % 4, t0=tok0:
                        e.dma_start(rxt[p][:], src[h, :, t0:t0 + 512]),
                        waits=[t_ev], inc=s_rx[(ht + 4) % 4], k=16)
                if not final_pass:
                    t_xn_st[par] = pb.op(
                        "sync", lambda e, h=ht, p=par, t0=tok0: e.dma_start(
                            dst[h, :, t0:t0 + 512], xnew[p][:]),
                        waits=[t_ev], inc=s_xst[par], k=16)
                else:
                    def emit_tr(h2, p2, ev2, b2):
                        for tt in range(4):
                            pbk = tt % 2
                            slot = (h2 * 4 + tt) % 4
                            t_tr = pb.op(
                                "tensor", lambda e, p=p2, t=tt, b=pbk:
                                e.transpose(bank(b, 128),
                                            xnew[p][:, t * 128:(t + 1) * 128],
                                            cst["ident_f"][:]),
                                waits=[ev2] +
                                (t_tr_ev[-2:-1] if len(t_tr_ev) >= 2 else []),
                                inc=STn)
                            t_trs.append(t_tr)
                            xnw_tr_user[p2] = t_tr
                            t_oe = pb.op("scalar",
                                         lambda e, sl=slot, b=pbk: e.copy(
                                             stg[sl][:], bank(b, 128)),
                                         waits=[t_tr, t_onat_st[slot]], inc=SS)
                            t_tr_ev.append(t_oe)
                            t_onat_st[slot] = pb.op(
                                "sync", lambda e, sl=slot, h=h2,
                                r=b2 * 4 + tt: e.dma_start(
                                    out_nat[r * 128:(r + 1) * 128,
                                            h * 128:(h + 1) * 128], stg[sl][:]),
                                waits=[t_oe], inc=s_on[slot], k=16)
                    if pend_tr[0] is not None:
                        emit_tr(*pend_tr[0])
                    pend_tr[0] = (ht, par, t_ev, blk)
            le = c.HK - 1 if (c.HK - 1) % 2 == 0 else c.HK - 2
            lo = c.HK - 1 if (c.HK - 1) % 2 == 1 else c.HK - 2
            pair_sil[2] = t_dn_ev[le]
            pair_fus[2] = t_dn_ev[lo] if lo >= 0 else None

        if final_pass and pend_tr[0] is not None:
            h2, p2, ev2, b2 = pend_tr[0]
            for tt in range(4):
                pbk = tt % 2
                slot = (h2 * 4 + tt) % 4
                t_tr = pb.op("tensor", lambda e, p=p2, t=tt, b=pbk:
                             e.transpose(bank(b, 128),
                                         xnew[p][:, t * 128:(t + 1) * 128],
                                         cst["ident_f"][:]),
                             waits=[ev2] + t_tr_ev[-2:-1], inc=STn)
                t_oe = pb.op("scalar", lambda e, sl=slot, b=pbk: e.copy(
                    stg[sl][:], bank(b, 128)),
                    waits=[t_tr, t_onat_st[slot]], inc=SS)
                t_tr_ev.append(t_oe)
                t_onat_st[slot] = pb.op(
                    "sync", lambda e, sl=slot, h=h2, r=b2 * 4 + tt:
                    e.dma_start(out_nat[r * 128:(r + 1) * 128,
                                        h * 128:(h + 1) * 128], stg[sl][:]),
                    waits=[t_oe], inc=s_on[slot], k=16)
        ar.free()

    def emit_attn(src, dst):
        NH, SK, HK, NG = c.NH, c.SK, c.HK, c.NG
        NCH = c.S // 512
        NWQ = 3
        g_rd = {i: [] for i in range(7)}
        g_vps = []
        g_aT = {0: [], 1: []}
        natc = [0]
        stc = [0]
        SB3 = (0, 1)

        ar = mk_arena()
        xln_at2 = [ar.sbuf_tensor(f"at_xln{bb}", [128, HK, c.S], BF16)
                   for bb in range(c.BPC)]
        wqb = [ar.sbuf_tensor(f"at_wq{i}", [128, HK, 128], BF16)
               for i in range(NWQ)]
        qst = [ar.sbuf_tensor(f"at_qst{i}", [128, 512], BF16)
               for i in range(4)]
        ar.allocate()
        s_wq = [mkds(f"s_wq{i}", "gpsimd") for i in range(NWQ)]
        s_qst = [mkds(f"s_qst{i}", "sync") for i in range(4)]

        xln_tick = {}
        ready = set()
        ln_count = [0]

        def emit_attn_ln(bb, cc, split=False):
            lb = 5 + ln_count[0] % 2
            ln_count[0] += 1
            r = emit_ln(src, bb * c.S + cc * 512,
                        lambda h, x=bb, c0=cc:
                        xln_at2[x][:, h, c0 * 512:(c0 + 1) * 512],
                        lbank=lb, split=split)
            if split:
                def fin(r=r, bb=bb, cc=cc, lb=lb):
                    tk, tc2 = r()
                    xln_tick[(bb, cc)] = tk
                    g_rd[lb].append(tc2)
                return fin
            tk, tc2 = r
            xln_tick[(bb, cc)] = tk
            g_rd[lb].append(tc2)

        qkc = [0]
        t_qst = {}
        qkv_done = {}
        ring_last = {}

        def emit_slab(h, p, pairs, ws):
            fk = p * NH + h
            t_w = pb.op("gpsimd", lambda e, f=fk, w=ws: e.dma_start(
                wqb[w][:], di["wqkv_t"][f]),
                waits=[ring_last.get(ws)], inc=s_wq[ws], k=16)
            mm = None
            for (bb, cc) in pairs:
                bq = qkc[0] % 5
                qkc[0] += 1
                qs = qkc[0] % 4
                tk = xln_tick[(bb, cc)]
                for hk in range(HK):
                    w0 = []
                    if hk == 0:
                        w0 = [t_w] + g_rd[bq][-1:]
                    w0 = w0 + [tk[hk]]
                    mm = pb.op("tensor",
                               lambda e, w=ws, k=hk, b=bq, c0=cc, x=bb:
                               e.matmul(bank(b), wqb[w][:, k, :],
                                        xln_at2[x][:, k,
                                                   c0 * 512:(c0 + 1) * 512],
                                        start=(k == 0),
                                        stop=(k == HK - 1)),
                               waits=w0, inc=STn)
                tev = pb.op("scalar",
                            lambda e, f=fk, b=bq, q=qs:
                            e.activation(
                                qst[q][:], bank(b), AF.Identity,
                                bias=cst["qkvb_c"][:, f:f + 1]),
                            waits=[mm, t_qst.get(qs)], inc=SS)
                g_rd[bq].append(tev)
                t_qst[qs] = pb.op(
                    "sync", lambda e, hh=bb * NH + h, pp=p, c0=cc, q=qs:
                    e.dma_start(
                        qkvT_d[hh, pp][:, c0 * 512:(c0 + 1) * 512],
                        qst[q][:]), waits=[tev], inc=s_qst[qs], k=16)
            qkv_done[(h, p)] = mm
            ring_last[ws] = mm

        pend = {}
        allc = [(bb, cc) for bb in range(c.BPC) for cc in range(NCH)]
        n_up = min(2, NCH)
        for cc in range(n_up):
            emit_attn_ln(0, cc)
            ready.add((0, cc))
        rem = allc[n_up:]
        stats_sched = {2 + 2 * k: rem[k] for k in range(len(rem))}
        fin_sched = {3 + 2 * k: rem[k] for k in range(len(rem))}
        deferred = []
        si = 0
        for h in range(NH):
            for p in range(3):
                if si in stats_sched:
                    bb, cc = stats_sched[si]
                    pend[(bb, cc)] = emit_attn_ln(bb, cc, split=True)
                if si in fin_sched:
                    bb, cc = fin_sched[si]
                    pend.pop((bb, cc))()
                    ready.add((bb, cc))
                pairs = [pr for pr in allc if pr in ready]
                miss = [pr for pr in allc if pr not in ready]
                emit_slab(h, p, pairs, si % NWQ)
                if miss:
                    deferred.append((h, p, miss))
                si += 1
        for pr in rem:
            if pr in pend:
                pend.pop(pr)()
                ready.add(pr)
            elif pr not in ready:
                emit_attn_ln(*pr)
                ready.add(pr)
        for (h, p, prs) in deferred:
            emit_slab(h, p, prs, si % NWQ)
            si += 1
        dense_tail = [t_qst.get(i) for i in range(4)]
        barrier()
        ar.free()

        ar = mk_arena()
        qT = [ar.sbuf_tensor(f"at_q{i}", [128, c.S], BF16) for i in range(2)]
        kT = [ar.sbuf_tensor(f"at_k{i}", [128, c.S], BF16) for i in range(2)]
        vT = [ar.sbuf_tensor("at_v0", [128, c.S], BF16)] * 2
        Vb = [ar.sbuf_tensor(f"at_V{i}", [128, SK, 128], BF16)
              for i in range(2)]
        attnT = [ar.sbuf_tensor("at_aT0", [128, SK, 512], BF16)] * 2
        ctxT_h = ar.sbuf_tensor("at_ctx", [128, c.S], BF16)
        m_all = ar.sbuf_tensor("at_mall", [128, 2 * SK], F32)
        nb_bf = ar.sbuf_tensor("at_nbbf", [128, SK], BF16)
        nb_f = ar.sbuf_tensor("at_nbf", [128, SK], F32)
        cmx = ar.sbuf_tensor("at_cmx", [128, 4], F32)
        lps = ar.sbuf_tensor("at_lps", [128, 4], F32)
        dscr = ar.sbuf_tensor("at_dscr", [128, 512], BF16)
        nmT = ar.sbuf_tensor("at_nmT", [2 * SK, 128], BF16)
        nrow2 = ar.sbuf_tensor("at_nr2", [1, 2 * c.S], BF16)
        rl_bc = ar.sbuf_tensor("at_rlbc", [128, 512], F32)
        ctx_q = [ar.sbuf_tensor(f"at_cq{i}", [128, 1024], BF16) for i in range(3)]
        wos = [ar.sbuf_tensor(f"at_wo{i}", [128, NH, 128], BF16)
               for i in range(4)]
        rxo = [ar.sbuf_tensor(f"at_rxo{i}", [128, 512], F32) for i in range(2)]
        x2w = [ar.sbuf_tensor(f"at_x2w{i}", [128, 512], F32) for i in range(2)]
        arh = [ar.sbuf_tensor(f"at_arh{i}", [1, c.S], BF16) for i in range(2)]
        arl2 = [ar.sbuf_tensor(f"at_arl2{i}", [1, c.S], BF16)
                for i in range(2)]
        ar.allocate()
        s_wo = [mkds(f"s_wo{i}", "gpsimd") for i in range(4)]
        s_cq = [mkds(f"s_cq{i}", "sync") for i in range(3)]
        s_rxo = [mkds(f"s_rxo{i}", "sync") for i in range(2)]
        s_x2 = [mkds(f"s_x2{i}", "sync") for i in range(2)]
        s_ctxst = mkds("s_ctxst", "sync")
        s_b1 = mkds("s_b1", "sync")
        s_b2 = mkds("s_b2", "sync")
        s_ar = [mkds(f"s_ar{i}", "sync") for i in range(4)]
        s_ql = [mkds(f"s_ql{i}", "sync") for i in range(2)]
        s_kl = [mkds(f"s_kl{i}", "sync") for i in range(2)]
        s_vl = mkds("s_vl", "sync")

        for b in range(c.BPC):
            s0 = b * c.S
            H = [dict() for _ in range(NH)]

            def load_gen(h):
                st = H[h]
                par = h % 2
                pw = H[h - 1] if h > 0 else {}
                pw2 = H[h - 2] if h > 1 else {}
                qkv_tail = dense_tail
                st["ql"] = pb.op("sync", lambda e, hh=b * NH + h, p=par:
                    e.dma_start(qT[p][:], qkvT_d[hh, 0]),
                    waits=qkv_tail + [pw2.get("last_sT_mm"),
                                      pw2.get("last_nat_mm")],
                    inc=s_ql[par], k=16)
                st["kl"] = pb.op("sync", lambda e, hh=b * NH + h, p=par:
                    e.dma_start(kT[p][:], qkvT_d[hh, 1]),
                    waits=qkv_tail + [pw2.get("last_sT_mm"),
                                      pw2.get("last_nat_mm")],
                    inc=s_kl[par], k=16)
                st["vl"] = pb.op("sync", lambda e, hh=b * NH + h:
                    e.dma_start(vT[0][:], qkvT_d[hh, 2]),
                    waits=qkv_tail + [pw.get("last_vtr")], inc=s_vl, k=16)
                st["ar_ld"] = pb.op(
                    "sync", lambda e, hh=h, p=par: e.dma_start(
                        arh[p][:], di["arow_hi"][hh:hh + 1, :]),
                    waits=[pw2.get("last_nat_mm")], inc=s_ar[2 * par], k=16)
                st["ar_ld2"] = pb.op(
                    "sync", lambda e, hh=h, p=par: e.dma_start(
                        arl2[p][:], di["arow_lo"][hh:hh + 1, :]),
                    waits=[pw2.get("last_nat_mm")],
                    inc=s_ar[2 * par + 1], k=16)
                st["q"] = [st["ql"]]
                st["k"] = [st["kl"]]
                if False:
                    yield

            def vtrans_gen(h):
                st = H[h]
                par = h % 2
                st["Ve"] = evs = []
                last_tr = None
                for jt in range(SK):
                    slot = jt % 4
                    t_tr = pb.op("tensor", lambda e, j=jt, s=slot:
                                 e.transpose(ps_bf[:, s * 128:(s + 1) * 128],
                                             vT[0][:, j * 128:(j + 1) * 128],
                                             cst["ident_bf"][:]),
                                 waits=[st["vl"]] + (evs[-1:] if evs else
                                                     g_vps[-1:]),
                                 inc=STn)
                    t_ev = pb.op("scalar", lambda e, j=jt, p=par, s=slot: e.copy(
                        Vb[p][:, j, :], ps_bf[:, s * 128:(s + 1) * 128]),
                        waits=[t_tr], inc=SS)
                    g_vps.append(t_ev)
                    evs.append(t_ev)
                    last_tr = t_tr
                    st["last_vtr"] = last_tr
                    yield

            def natural_gen(h):
                st = H[h]
                par = h % 2
                pw = H[h - 1] if h > 0 else {}
                t_red_all = []
                t_l_all = []
                last_mm = None
                for it in range(SK):
                    jext = (it + 1) * 128
                    nch = (jext + 511) // 512
                    t_its = []
                    banks_used = []
                    for ch in range(nch):
                        j0 = ch * 512
                        w = min(512, jext - j0)
                        bq = 2 + natc[0] % 4
                        natc[0] += 1
                        banks_used.append((bq, w))
                        diag = (j0 + w == jext)
                        pb.op("tensor", lambda e, i=it, j=j0, ww=w, b=bq,
                              p=par: e.matmul(
                                  bank(b, ww),
                                  qT[p][:, i * 128:(i + 1) * 128],
                                  kT[p][:, j:j + ww], start=True, stop=False),
                              waits=st["q"] + st["k"] + g_rd[bq][-1:] +
                              [st["ar_ld"], st["ar_ld2"]], inc=STn)
                        pb.op("tensor", lambda e, j=j0, ww=w, b=bq, p=par:
                              e.matmul(bank(b, ww), cst["ones_row_bf"][:],
                                       arh[p][0:1, j:j + ww],
                                       start=False, stop=False), inc=STn)
                        mm = pb.op("tensor", lambda e, j=j0, ww=w, b=bq, p=par:
                                   e.matmul(bank(b, ww), cst["ones_row_bf"][:],
                                            arl2[p][0:1, j:j + ww],
                                            start=False, stop=True), inc=STn)
                        last_mm = mm
                        wred = [mm] + (t_red_all[-1:] if ch == 0 else [])
                        if diag:
                            tdm = pb.op("vector", lambda e, b=bq, ww=w:
                                        e.tensor_add(
                                            bank(b, ww)[:, ww - 128:ww],
                                            bank(b, ww)[:, ww - 128:ww],
                                            cst["tri_n"][:]),
                                        waits=[mm], inc=SV)
                            wred.append(tdm)
                        tr = pb.op("vector", lambda e, b=bq, ww=w, ch_=ch:
                                   e.reduce_max(cmx[:, ch_:ch_ + 1],
                                                bank(b, ww), AX.X),
                                   waits=wred, inc=SV)
                        t_its.append(tr)
                    t_comb = pb.op(
                        "vector", lambda e, i=it, n=nch: e.reduce_max(
                            m_all[:, i:i + 1], cmx[:, 0:n], AX.X,
                            negate=True),
                        waits=t_its + [pw.get("m_tr")], inc=SV)
                    t_red_all.append(t_comb)
                    tb1 = pb.op("vector", lambda e, i=it: e.tensor_copy(
                        nb_bf[:, i:i + 1], m_all[:, i:i + 1]),
                        waits=[t_comb], inc=SV)
                    tb2 = pb.op("vector", lambda e, i=it: e.tensor_copy(
                        nb_f[:, i:i + 1], nb_bf[:, i:i + 1]),
                        waits=[tb1], inc=SV)
                    t_exps = []
                    for ch in range(nch):
                        bq, w = banks_used[ch]
                        te = pb.op("scalar", lambda e, b=bq, ww=w, i=it, c_=ch:
                                   e.activation(
                                       dscr[:, 0:ww], bank(b, ww), AF.Exp,
                                       bias=nb_f[:, i:i + 1],
                                       accum_out=lps[:, c_:c_ + 1]),
                                   waits=[tb2] +
                                   (t_exps[-1:] if ch else t_l_all[-1:]),
                                   inc=SS)
                        g_rd[bq].append(te)
                        t_exps.append(te)
                    t_l_all.append(pb.op(
                        "vector", lambda e, i=it, n=nch: e.reduce_sum(
                            m_all[:, SK + i:SK + i + 1], lps[:, 0:n], AX.X),
                        waits=t_exps, inc=SV))
                    st["last_nat_mm"] = last_mm
                    yield
                st["t_l_all"] = t_l_all

            def natural_tail(h):
                st = H[h]
                pw = H[h - 1] if h > 0 else {}
                t_rc = pb.op("vector", lambda e: e.reciprocal(
                    m_all[:, SK:2 * SK], m_all[:, SK:2 * SK]),
                    waits=st["t_l_all"], inc=SV)
                t_tr = pb.op("tensor", lambda e: e.transpose(
                    ps[0:2 * SK, 1024:1152], m_all[:], cst["ident_f"][:]),
                    waits=[t_rc] + g_rd[2][-1:], inc=STn)
                st["m_tr"] = t_tr
                t_ev = pb.op("scalar", lambda e: e.copy(
                    nmT[:], ps[0:2 * SK, 1024:1152]),
                    waits=[t_tr, pw.get("b1")], inc=SS)
                g_rd[2].append(t_ev)
                t_b1 = pb.op("sync", lambda e: e.dma_start(
                    bounce_d.reshape([2 * SK, 128])[:], nmT[:]),
                    waits=[t_ev, pw.get("b2")], inc=s_b1, k=16)
                st["b1"] = t_b1
                t_b2 = pb.op("sync", lambda e: e.dma_start(
                    nrow2[:], bounce_d[:]),
                    waits=[t_b1, pw.get("last_rank1"), pw.get("last_rlb")],
                    inc=s_b2, k=16)
                st["b2"] = t_b2
                st["cast"] = t_b2

            def groups_gen(h):
                st = H[h]
                par = h % 2
                pw = H[h - 1] if h > 0 else {}
                t_ctx_done = []
                for g in range(NG):
                    gpar = g % 2
                    t_exps = []
                    aT_w = g_aT[0][-1:]
                    for jt in range(4 * g + 4):
                        sb_ = SB3[stc[0] % 2]
                        stc[0] += 1
                        lo = max(0, (jt - 4 * g)) * 128
                        mm = pb.op("tensor", lambda e, j=jt, b=sb_, p=par, g_=g:
                                   e.matmul(bank(b),
                                            kT[p][:, j * 128:(j + 1) * 128],
                                            qT[p][:, g_ * 512:(g_ + 1) * 512],
                                            start=True, stop=False),
                                   waits=[st.get("cast")] + st["q"] + st["k"] +
                                   g_rd[sb_][-1:], inc=STn)
                        st["last_sT_mm"] = mm
                        mm2 = pb.op("tensor", lambda e, b=sb_, g_=g: e.matmul(
                            bank(b), cst["ones_row_bf"][:],
                            nrow2[0:1, g_ * 512:(g_ + 1) * 512],
                            start=False, stop=True), inc=STn)
                        st["last_rank1"] = mm2
                        tw = [mm2]
                        if jt >= 4 * g:
                            dt_ = pb.op("vector",
                                        lambda e, b=sb_, l=lo: e.tensor_add(
                                            bank(b)[:, l:l + 128],
                                            bank(b)[:, l:l + 128],
                                            cst["tri_t"][:]),
                                        waits=[mm2], inc=SV)
                            tw = [dt_]
                        if lo > 0:
                            tms = pb.op("vector", lambda e, j=jt, l=lo:
                                        e.memset(attnT[0][:, j, 0:l], 0),
                                        waits=aT_w, inc=SV)
                            tw.append(tms)
                        t_e = pb.op("scalar", lambda e, j=jt, b=sb_, l=lo,
                                    hh=h: e.activation(
                                        attnT[0][:, j, l:512],
                                        bank(b)[:, l:512], AF.Exp,
                                        bias=cst["alibi_c"][:,
                                                            hh * SK + j:
                                                            hh * SK + j + 1]),
                                    waits=tw + aT_w, inc=SS)
                        g_rd[sb_].append(t_e)
                        t_exps.append(t_e)
                        yield
                    bb = SB3[stc[0] % 2]
                    stc[0] += 1
                    t_rb = pb.op("tensor", lambda e, b=bb, g_=g: e.matmul(
                        bank(b), cst["ones_row_bf"][:],
                        nrow2[0:1, c.S + g_ * 512:c.S + (g_ + 1) * 512],
                        start=True, stop=True),
                        waits=[st.get("cast")] + g_rd[bb][-1:], inc=STn)
                    st["last_rlb"] = t_rb
                    t_rc = pb.op("vector", lambda e, b=bb: e.tensor_copy(
                        rl_bc[:], bank(b)), waits=[t_rb] + t_ctx_done[-1:],
                        inc=SV)
                    g_rd[bb].append(t_rc)
                    t_cmm = None
                    for jt in range(4 * g + 4):
                        t_cmm = pb.op("tensor", lambda e, j=jt, p=par,
                                      f=(jt == 0), la=(jt == 4 * g + 3):
                                      e.matmul(bank(6), Vb[p][:, j, :],
                                               attnT[0][:, j, :],
                                               start=f, stop=la),
                                      waits=(t_exps + g_rd[6][-1:] +
                                             H[h]["Ve"])
                                      if jt == 0 else (), inc=STn)
                        g_aT[0].append(t_cmm)
                    st["last_ctx_mm"] = t_cmm
                    t_cev = pb.op("vector", lambda e, g_=g: e.tensor_mul(
                        ctxT_h[:, g_ * 512:(g_ + 1) * 512], bank(6), rl_bc[:]),
                        waits=[t_cmm, t_rc, pw.get("ctx_store")], inc=SV)
                    g_rd[6].append(t_cev)
                    t_ctx_done.append(t_cev)
                    yield
                st["ctx_store"] = pb.op("sync", lambda e, hh=h: e.dma_start(
                    ctxT_d[hh], ctxT_h[:]), waits=t_ctx_done, inc=s_ctxst, k=16)

            from itertools import chain as _chain

            def drive(a_gens, b_gens):
                ita = _chain(*a_gens)
                itb = _chain(*b_gens)
                alive_a = alive_b = True
                turn_a = True
                while alive_a or alive_b:
                    it = ita if (turn_a and alive_a) or not alive_b else itb
                    try:
                        next(it)
                    except StopIteration:
                        if it is ita:
                            alive_a = False
                        else:
                            alive_b = False
                    turn_a = not turn_a

            SKIP = os.environ.get("K_SKIP", "")
            list(load_gen(0))
            if "nat" not in SKIP:
                list(natural_gen(0))
                natural_tail(0)
            list(vtrans_gen(0))
            for h in range(NH):
                if h + 1 < NH:
                    list(load_gen(h + 1))
                    drive([groups_gen(h)] if "grp" not in SKIP else [],
                          [natural_gen(h + 1)] if "nat" not in SKIP else [])
                    if "nat" not in SKIP:
                        natural_tail(h + 1)
                    list(vtrans_gen(h + 1))
                else:
                    if "grp" not in SKIP:
                        drive([groups_gen(h)], [])
            ctx_stores = [H[h].get("ctx_store") for h in range(NH)]
            barrier()
            QS = min(2, HK)
            P2 = min(2, NCH)
            prev_q_mm = {}
            prev_ev = {}
            prev_st = {}
            prev_cx = {}
            for q in range(HK // QS):
                t_wo = []
                for i in range(QS):
                    hk = q * QS + i
                    sl = (q % 2) * QS + i
                    t_wo.append(pb.op("gpsimd", lambda e, k=hk, i_=sl:
                                      e.dma_start(wos[i_][:], di["wout_t"][k]),
                                      waits=prev_q_mm.get(q - 2, []),
                                      inc=s_wo[sl], k=16))
                mmq = {}
                t_cx_all = {}
                seq = [(chp, fk) for chp in range(NCH // P2)
                       for fk in range(NH)]
                for si in range(min(3, len(seq))):
                    chp, fk = seq[si]
                    t_cx_all[si] = pb.op(
                        "sync", lambda e, f=fk, c0=chp, p=si % 3:
                        e.dma_start(ctx_q[p][:, 0:P2 * 512],
                                    ctxT_d[f][:, c0 * P2 * 512:
                                              (c0 + 1) * P2 * 512]),
                        waits=prev_cx.get(si % 3, []), inc=s_cq[si % 3], k=16)
                for si, (chp, fk) in enumerate(seq):
                    cpar = si % 3
                    t_cx = t_cx_all[si]
                    for sub in range(P2):
                        for i in range(QS):
                            bq = sub * QS + i
                            mmq[(sub, i)] = pb.op(
                                "tensor", lambda e, i_=(q % 2) * QS + i, f=fk,
                                p=cpar, sb=sub,
                                b=bq: e.matmul(
                                    bank(b), wos[i_][:, f, :],
                                    ctx_q[p][:, sb * 512:(sb + 1) * 512],
                                    start=(f == 0), stop=(f == NH - 1)),
                                waits=([t_wo[i], t_cx] +
                                       prev_ev.get((sub, i), []))
                                if fk == 0 else [t_cx], inc=STn)
                    last_mm = mmq[(P2 - 1, QS - 1)]
                    if si + 3 < len(seq):
                        ch2, fk2 = seq[si + 3]
                        t_cx_all[si + 3] = pb.op(
                            "sync", lambda e, f=fk2, c0=ch2, p=(si + 3) % 3:
                            e.dma_start(ctx_q[p][:, 0:P2 * 512],
                                        ctxT_d[f][:, c0 * P2 * 512:
                                                  (c0 + 1) * P2 * 512]),
                            waits=[last_mm], inc=s_cq[(si + 3) % 3], k=16)
                    if fk == NH - 1:
                        for sub in range(P2):
                            ch = chp * P2 + sub
                            for i in range(QS):
                                hk = q * QS + i
                                rp = (ch * QS + i) % 2
                                t_rx = pb.op(
                                    "sync", lambda e, k=hk, c0=ch, p=rp,
                                    s_=s0: e.dma_start(
                                        rxo[p][:],
                                        src[k, :, s_ + c0 * 512:
                                            s_ + (c0 + 1) * 512]),
                                    waits=prev_ev.get(("r", rp), []),
                                    inc=s_rxo[rp], k=16)
                                t_ev = pb.op(
                                    "vector", lambda e, k=hk, b=sub * QS + i,
                                    p=rp: e.scalar_tensor_tensor(
                                        x2w[p][:], bank(b),
                                        cst["outb_c"][:, k:k + 1], rxo[p][:],
                                        ALU.add, ALU.add),
                                    waits=[mmq[(sub, i)], t_rx,
                                           prev_st.get(rp)], inc=SV)
                                prev_ev[(sub, i)] = [t_ev]
                                prev_ev[("r", rp)] = [t_ev]
                                prev_st[rp] = pb.op(
                                    "sync", lambda e, k=hk, c0=ch, p=rp,
                                    s_=s0: e.dma_start(
                                        dst[k, :, s_ + c0 * 512:
                                            s_ + (c0 + 1) * 512],
                                        x2w[p][:]), waits=[t_ev],
                                    inc=s_x2[rp], k=16)
                prev_q_mm[q] = [mmq[(sb, i)] for sb in range(P2)
                                for i in range(QS)]
                prev_cx = {i: [last_mm] for i in range(3)}
            barrier()
        ar.free()

    emit_ffn(xT_a, xT_b, final_pass=False)
    barrier()
    emit_attn(xT_b, xT_a)
    barrier()
    emit_ffn(xT_a, None, final_pass=True)
    barrier()
    pb.op("sync", lambda e: e.nop())

    with nc.Block() as block:
        @block.tensor
        def _(e):
            for f in pb.q["tensor"]:
                f(e)

        @block.scalar
        def _(e):
            for f in pb.q["scalar"]:
                f(e)

        @block.vector
        def _(e):
            for f in pb.q["vector"]:
                f(e)

        @block.gpsimd
        def _(e):
            for f in pb.q["gpsimd"]:
                f(e)

        @block.sync
        def _(e):
            for f in pb.q["sync"]:
                f(e)

    return nc


def kernel(**inputs):
    cfg = Cfg()
    g, xs = host_prep(cfg, inputs)
    nc = build_program(cfg)
    in_maps = []
    for cid in range(cfg.n_cores):
        m = dict(g)
        m["x_nat"] = xs[cid]
        in_maps.append(m)
    res = run_bass_kernel_spmd(nc, in_maps, core_ids=list(range(cfg.n_cores)))
    outs = [np.asarray(res.results[cid]["out_nat"], dtype=np.float32)
            .reshape(cfg.BPC, cfg.S, cfg.H) for cid in range(cfg.n_cores)]
    return np.concatenate(outs, axis=0)


# version 27
# speedup vs baseline: 1.0539x; 1.0539x over previous
import os
import numpy as np
import ml_dtypes
import concourse.bass as bass
import concourse.mybir as mybir
from concourse.bass_utils import run_bass_kernel_spmd

F32 = mybir.dt.float32
BF16 = mybir.dt.bfloat16
I8 = mybir.dt.int8
AF = mybir.ActivationFunctionType
ALU = mybir.AluOpType
AX = mybir.AxisListType

ENGS = ("tensor", "scalar", "vector", "gpsimd", "sync")
NEG = -1.0e30


class Cfg:
    def __init__(self, B=16, S=2048, H=2048, NH=16, I=8192, n_cores=8):
        self.B, self.S, self.H, self.NH, self.I = B, S, H, NH, I
        self.n_cores = n_cores
        self.BPC = B // n_cores
        self.T = self.BPC * S
        self.HK = H // 128
        self.IK = I // 128
        self.SK = S // 128
        self.FK = 3 * NH
        self.TB = 512
        self.NTB = self.T // self.TB
        self.NG = S // 512
        assert S % 512 == 0 and H % 128 == 0 and I % 128 == 0


class Sem:
    def __init__(self, h):
        self.h = h
        self.n = 0


class PB:

    def __init__(self):
        self.q = {e: [] for e in ENGS}
        self.mark = {e: {} for e in ENGS}
        self.pending_barrier = {e: None for e in ENGS}
        self.dma_reg = []

    def op(self, eng, fn, waits=(), inc=None, k=1):
        if self.pending_barrier[eng] is not None:
            waits = list(waits) + self.pending_barrier[eng]
            self.pending_barrier[eng] = None
        wd = {}
        for w in waits:
            if w is None:
                continue
            s, v = w
            if v and (s not in wd or v > wd[s]):
                wd[s] = v
        m = self.mark[eng]
        todo = []
        for s, v in wd.items():
            if m.get(s, 0) < v:
                todo.append((s.h, v))
                m[s] = v
        todo = tuple(todo)
        tick = None
        if inc is not None:
            inc.n += k
            tick = (inc, inc.n)

        def run(e, _w=todo, _fn=fn, _inc=inc, _k=k):
            for (h, v) in _w:
                e.wait_ge(h, v)
            ins = _fn(e)
            if _inc is not None:
                ins.then_inc(_inc.h, _k)

        self.q[eng].append(run)
        return tick

    def barrier(self, csems, fence_sems):
        sb_s, sb_g = fence_sems
        t_s = self.op("sync", lambda e: e.nop(),
                      waits=[(s, s.n) for (s, q) in self.dma_reg
                             if q == "sync" and s.n > 0], inc=sb_s)
        t_g = self.op("gpsimd", lambda e: e.nop(),
                      waits=[(s, s.n) for (s, q) in self.dma_reg
                             if q == "gpsimd" and s.n > 0], inc=sb_g)
        snap = [t_s, t_g] + [(s, s.n) for s in csems if s.n > 0]
        for e in ENGS:
            self.pending_barrier[e] = list(snap)


def host_prep(cfg, inputs):
    c = cfg
    g = {}
    xw = np.ascontiguousarray(np.asarray(inputs["L0_attn_qkv_w"], dtype=np.float32))
    g["wqkv_t"] = np.ascontiguousarray(
        xw.reshape(c.HK, 128, c.FK, 128).transpose(2, 1, 0, 3))
    ow = np.ascontiguousarray(np.asarray(inputs["L0_attn_out_w"], dtype=np.float32))
    g["wout_t"] = np.ascontiguousarray(
        ow.reshape(c.NH, 128, c.HK, 128).transpose(2, 1, 0, 3))
    wg = np.asarray(inputs["L0_ffn_w_gate"])
    wu = np.asarray(inputs["L0_ffn_w_up"])
    wd = np.asarray(inputs["L0_ffn_w_down"])
    g["wg_t"] = np.ascontiguousarray(
        wg.reshape(c.HK, 128, c.IK, 128).transpose(2, 1, 0, 3))
    g["wu_t"] = np.ascontiguousarray(
        wu.reshape(c.HK, 128, c.IK, 128).transpose(2, 1, 0, 3))
    g["wd_t"] = np.ascontiguousarray(
        wd.reshape(c.IK, 128, c.HK, 128).transpose(2, 1, 0, 3))
    g["qkvb_c"] = np.ascontiguousarray(
        np.asarray(inputs["L0_attn_qkv_b"], dtype=np.float32).reshape(c.FK, 128).T)
    g["outb_c"] = np.ascontiguousarray(
        np.asarray(inputs["L0_attn_out_b"], dtype=np.float32).reshape(c.HK, 128).T)
    g["sg_c"] = np.ascontiguousarray(
        np.asarray(inputs["L0_ffn_s_gate"], dtype=np.float32).reshape(c.IK, 128).T)
    g["su_c"] = np.ascontiguousarray(
        np.asarray(inputs["L0_ffn_s_up"], dtype=np.float32).reshape(c.IK, 128).T)
    g["sd_c"] = np.ascontiguousarray(
        np.asarray(inputs["L0_ffn_s_down"], dtype=np.float32).reshape(c.HK, 128).T)
    g["ident_bf"] = np.eye(128, dtype=ml_dtypes.bfloat16)
    g["ident_f"] = np.eye(128, dtype=np.float32)
    g["ones_col_bf"] = np.ones((128, 1), dtype=ml_dtypes.bfloat16)
    g["ones_row_bf"] = np.ones((1, 128), dtype=ml_dtypes.bfloat16)
    g["ones_row_f"] = np.ones((1, 128), dtype=np.float32)
    jj = np.arange(128)[:, None]
    ii = np.arange(128)[None, :]
    g["tri_t"] = np.where(jj > ii, np.float32(NEG), np.float32(0.0)).astype(np.float32)
    if c.NH <= 8:
        slopes = np.array([1.0 / 2.0 ** i for i in range(c.NH)], dtype=np.float32)
    else:
        slopes = np.array([1.0 / 2.0 ** (i / 2.0) for i in range(c.NH)],
                          dtype=np.float32)
    al = np.zeros((128, c.NH * c.SK), dtype=np.float32)
    for h in range(c.NH):
        for jt in range(c.SK):
            al[:, h * c.SK + jt] = slopes[h] * (jt * 128 + np.arange(128))
    g["alibi_c"] = al
    arow = np.zeros((c.NH, c.S), dtype=np.float64)
    for h in range(c.NH):
        arow[h] = slopes[h].astype(np.float64) * np.arange(c.S)
    hi = arow.astype(ml_dtypes.bfloat16)
    lo = (arow - hi.astype(np.float64)).astype(ml_dtypes.bfloat16)
    g["arow_hi"] = np.ascontiguousarray(hi)
    g["arow_lo"] = np.ascontiguousarray(lo)
    g["tri_n"] = np.where(jj < ii, np.float32(NEG),
                          np.float32(0.0)).astype(np.float32)
    g["eps_c"] = np.full((1, 1), 1e-5, dtype=np.float32)
    x = np.asarray(inputs["L0_attn_x"], dtype=np.float32)
    xs = [np.ascontiguousarray(
        x[cid * c.BPC:(cid + 1) * c.BPC].reshape(c.T, c.H))
        for cid in range(c.n_cores)]
    return g, xs


def build_program(cfg, debug_outputs=False, use_arena=True):
    c = cfg
    nc = bass.Bass("TRN2", target_bir_lowering=False)
    pb = PB()

    di = {}
    def dram_in(name, shape, dt_):
        di[name] = nc.dram_tensor(name, list(shape), dt_, kind="ExternalInput")
    dram_in("x_nat", (c.T, c.H), F32)
    dram_in("wqkv_t", (c.FK, 128, c.HK, 128), F32)
    dram_in("wout_t", (c.HK, 128, c.NH, 128), F32)
    dram_in("wg_t", (c.IK, 128, c.HK, 128), I8)
    dram_in("wu_t", (c.IK, 128, c.HK, 128), I8)
    dram_in("wd_t", (c.HK, 128, c.IK, 128), I8)
    dram_in("qkvb_c", (128, c.FK), F32)
    dram_in("outb_c", (128, c.HK), F32)
    dram_in("sg_c", (128, c.IK), F32)
    dram_in("su_c", (128, c.IK), F32)
    dram_in("sd_c", (128, c.HK), F32)
    dram_in("ident_bf", (128, 128), BF16)
    dram_in("ident_f", (128, 128), F32)
    dram_in("ones_col_bf", (128, 1), BF16)
    dram_in("ones_row_bf", (1, 128), BF16)
    dram_in("ones_row_f", (1, 128), F32)
    dram_in("tri_t", (128, 128), F32)
    dram_in("alibi_c", (128, c.NH * c.SK), F32)
    dram_in("arow_hi", (c.NH, c.S), BF16)
    dram_in("arow_lo", (c.NH, c.S), BF16)
    dram_in("tri_n", (128, 128), F32)
    dram_in("eps_c", (1, 1), F32)

    dbg = "ExternalOutput" if debug_outputs else "Internal"
    out_nat = nc.dram_tensor("out_nat", [c.T, c.H], F32, kind="ExternalOutput")
    xT_a = nc.dram_tensor("xT_a", [c.HK, 128, c.T], F32, kind=dbg)
    xT_b = nc.dram_tensor("xT_b", [c.HK, 128, c.T], F32, kind=dbg)
    ctxT_d = nc.dram_tensor("ctxT_d", [c.NH, 128, c.S], BF16, kind=dbg)
    qkvT_d = nc.dram_tensor("qkvT_d", [c.BPC * c.NH, 3, 128, c.S], BF16,
                            kind="Internal")
    bounce_d = nc.dram_tensor("bounce_d", [1, 2 * c.S], BF16, kind="Internal")

    import contextlib
    es = contextlib.ExitStack()
    nc._keepalive = es
    STn = Sem(es.enter_context(nc.semaphore("STn")))
    SS = Sem(es.enter_context(nc.semaphore("SS")))
    SV = Sem(es.enter_context(nc.semaphore("SV")))
    SB_s = Sem(es.enter_context(nc.semaphore("SBs")))
    SB_g = Sem(es.enter_context(nc.semaphore("SBg")))
    CSEMS = (STn, SS, SV, SB_s, SB_g)

    def mkds(name, queue):
        sm = Sem(es.enter_context(nc.semaphore(name)))
        pb.dma_reg.append((sm, queue))
        return sm

    def barrier():
        pb.barrier(CSEMS, (SB_s, SB_g))

    def sb(name, shape, dt_):
        return es.enter_context(nc.sbuf_tensor(name, list(shape), dt_))

    class _FakeArena:
        def sbuf_tensor(self, name, shape, dt_):
            return sb(name, shape, dt_)

        def allocate(self):
            pass

        def free(self):
            pass

    def mk_arena():
        return nc.auto_arena() if use_arena else _FakeArena()

    cst = {}
    for n, shape, dt_ in (
        ("ident_bf", (128, 128), BF16), ("ident_f", (128, 128), F32),
        ("ones_col_bf", (128, 1), BF16), ("ones_row_bf", (1, 128), BF16),
        ("ones_row_f", (1, 128), F32), ("tri_t", (128, 128), F32),
        ("alibi_c", (128, c.NH * c.SK), F32), ("qkvb_c", (128, c.FK), F32),
        ("outb_c", (128, c.HK), F32), ("sg_c", (128, c.IK), F32),
        ("su_c", (128, c.IK), F32), ("sd_c", (128, c.HK), F32),
        ("tri_n", (128, 128), F32), ("eps_c", (1, 1), F32),
    ):
        cst[n] = sb("c_" + n, shape, dt_)
    half_sd = sb("half_sd", (128, c.HK), F32)

    ps = es.enter_context(nc.psum_tensor("ps", [128, 3584], F32))
    ps_bf = es.enter_context(nc.psum_tensor("ps_bf", [128, 1024], BF16))

    def bank(i, w=512):
        return ps[:, i * 512:i * 512 + w]

    s_cst = mkds("s_cst", "sync")
    t_cst = []
    for n in cst:
        t_cst.append(pb.op("sync", lambda e, a=cst[n], b=di[n]:
                           e.dma_start(a[:], b[:]), inc=s_cst, k=16))
    pb.op("vector", lambda e: e.tensor_scalar_mul(
        half_sd[:], cst["sd_c"][:], 0.5), waits=t_cst, inc=SV)
    barrier()

    ar0 = mk_arena()
    xnat_b = [ar0.sbuf_tensor(f"p0_xn{i}", [128, c.H], F32) for i in range(2)]
    xts_b = ar0.sbuf_tensor("p0_xts", [128, c.HK, 512], F32)
    ar0.allocate()
    s_xn = [mkds(f"s_xn{i}", "sync") for i in range(2)]
    s_xts = mkds("s_xts", "sync")
    t_evac_hist = []
    t_store = []
    for g in range(c.NTB):
        evac_this = []
        for tt in range(4):
            gt = g * 4 + tt
            par = gt % 2
            t_xn = pb.op(
                "sync", lambda e, p=par, r=gt: e.dma_start(
                    xnat_b[p][:], di["x_nat"][r * 128:(r + 1) * 128, :]),
                waits=t_evac_hist[-2 * c.HK:], inc=s_xn[par], k=16)
            for hk in range(c.HK):
                pbk = (tt * c.HK + hk) % 4
                wt = [t_xn]
                if len(t_evac_hist) >= 4:
                    wt.append(t_evac_hist[-4])
                t_tr = pb.op("tensor", lambda e, p=par, h=hk, b=pbk: e.transpose(
                    bank(b, 128), xnat_b[p][:, h * 128:(h + 1) * 128],
                    cst["ident_f"][:]), waits=wt, inc=STn)
                t_ev = pb.op("scalar", lambda e, h=hk, t=tt, b=pbk: e.copy(
                    xts_b[:, h, t * 128:(t + 1) * 128], bank(b, 128)),
                    waits=[t_tr] + (t_store[-1:] if t_store else []), inc=SS)
                t_evac_hist.append(t_ev)
                evac_this.append(t_ev)
        for q4 in range(0, c.HK, 4):
            nh = min(4, c.HK - q4)
            t_store.append(pb.op(
                "sync", lambda e, g_=g, q=q4, n=nh: e.dma_start(
                    xT_a.rearrange("k p t -> p k t")[:, q:q + n,
                                                     g_ * 512:(g_ + 1) * 512],
                    xts_b[:, q:q + n, :]),
                waits=evac_this, inc=s_xts, k=16))
    barrier()
    ar0.free()

    ln_xt = [sb(f"ln_xt{i}", (128, 512), F32) for i in range(4)]
    s_ln = [mkds(f"s_ln{i}", "sync") for i in range(4)]
    ln_xbf = [sb(f"ln_xbf{i}", (128, 512), BF16) for i in range(2)]
    ln_sq = [sb(f"ln_sq{i}", (128, 512), BF16) for i in range(2)]
    r_m = sb("r_m", (1, 512), F32)
    r_msq = sb("r_msq", (1, 512), F32)
    r_var = sb("r_var", (1, 512), F32)
    r_r = sb("r_r", (1, 512), F32)
    r_nmr = r_msq
    r_sd = r_var
    R_bc = sb("R_bc", (128, 512), F32)
    NMR_bc = sb("NMR_bc", (128, 512), F32)
    ln_state = {"apply": [], "b6_reader": {}, "rowmm": [],
                "slot_users": {}}

    def emit_ln(src, tok0, dst_slice_fn, dst_free=(), lbank=6, split=False):
        HK = c.HK
        sx = ps[0:1, lbank * 512:lbank * 512 + 512]
        sq = ps[32:33, lbank * 512:lbank * 512 + 512]
        t_mm_x = t_mm_q = None
        t_cast = {}
        b6w = [ln_state["b6_reader"].get(lbank)]
        for hk in range(HK):
            tp = hk % 4
            bp = hk % 2
            t_ld = pb.op("sync", lambda e, h=hk, p=tp, t0=tok0: e.dma_start(
                ln_xt[p][:], src[h, :, t0:t0 + 512]),
                waits=t_cast.get(hk - 4, []) +
                [ln_state["slot_users"].get(tp)], inc=s_ln[tp], k=16)
            t_c1 = pb.op("scalar", lambda e, p=tp, b=bp: e.copy(
                ln_xbf[b][:], ln_xt[p][:]),
                waits=[t_ld] + ([t_mm_x] if hk >= 2 else []), inc=SS)
            t_c2 = pb.op("scalar", lambda e, p=tp, b=bp: e.activation(
                ln_sq[b][:], ln_xt[p][:], AF.Square),
                waits=[t_ld] + ([t_mm_q] if hk >= 2 else []), inc=SS)
            t_cast[hk] = [t_c1, t_c2]
            t_mm_x = pb.op("tensor", lambda e, b=bp, h=hk: e.matmul(
                sx, cst["ones_col_bf"][:], ln_xbf[b][:],
                start=(h == 0), stop=(h == HK - 1)),
                waits=([t_c1] + b6w) if hk == 0 else [t_c1], inc=STn)
            t_mm_q = pb.op("tensor", lambda e, b=bp, h=hk: e.matmul(
                sq, cst["ones_col_bf"][:], ln_sq[b][:],
                start=(h == 0), stop=(h == HK - 1)), waits=[t_c2], inc=STn)
        inv = 1.0 / c.H
        rw = ln_state["rowmm"]
        t1 = pb.op("vector", lambda e: e.tensor_scalar_mul(r_m[:], sx, inv),
                   waits=[t_mm_x, t_mm_q] + rw, inc=SV)
        t2 = pb.op("vector", lambda e: e.tensor_scalar_mul(r_msq[:], sq, inv),
                   waits=[t_mm_q], inc=SV)
        t3 = pb.op("vector", lambda e: e.tensor_mul(r_var[:], r_m[:], r_m[:]),
                   waits=[t1], inc=SV)
        t4 = pb.op("vector", lambda e: e.tensor_sub(r_var[:], r_msq[:], r_var[:]),
                   waits=[t2, t3], inc=SV)
        t5 = pb.op("scalar", lambda e: e.activation(
            r_sd[:], r_var[:], AF.Sqrt, bias=cst["eps_c"][:]), waits=[t4], inc=SS)
        t6 = pb.op("vector", lambda e: e.reciprocal(r_r[:], r_sd[:]),
                   waits=[t5], inc=SV)
        t7 = pb.op("vector", lambda e: e.tensor_mul(r_nmr[:], r_m[:], r_r[:]),
                   waits=[t6], inc=SV)
        t8 = pb.op("vector", lambda e: e.tensor_scalar_mul(r_nmr[:], r_nmr[:], -1.0),
                   waits=[t7], inc=SV)
        def finish(tmx=t_mm_x, tmq=t_mm_q, tcst=t_cast):
            tb1 = pb.op("tensor", lambda e, lb=lbank: e.matmul(
                bank(lb), cst["ones_row_f"][:], r_r[:], start=True, stop=True),
                waits=[t6, t1, t2], inc=STn)
            tc1 = pb.op("vector", lambda e, lb=lbank: e.tensor_copy(
                R_bc[:], bank(lb)), waits=[tb1] + ln_state["apply"], inc=SV)
            tb2 = pb.op("tensor", lambda e, lb=lbank: e.matmul(
                bank(lb), cst["ones_row_f"][:], r_nmr[:], start=True, stop=True),
                waits=[t8, tc1], inc=STn)
            tc2 = pb.op("vector", lambda e, lb=lbank: e.tensor_copy(
                NMR_bc[:], bank(lb)), waits=[tb2], inc=SV)
            ln_state["b6_reader"][lbank] = tc2
            ln_state["rowmm"] = [tb1, tb2]
            out_ticks = []
            applies = []
            for hk in range(HK):
                tp = hk % 4
                t_ld = pb.op("sync", lambda e, h=hk, p=tp, t0=tok0: e.dma_start(
                    ln_xt[p][:], src[h, :, t0:t0 + 512]),
                    waits=(tcst.get(hk, []) +
                           ([tmx, tmq] if hk < 4 else []) +
                           (applies[-4:] if len(applies) >= 4 else [])),
                    inc=s_ln[tp], k=16)
                t_a1 = pb.op("vector", lambda e, p=tp: e.tensor_mul(
                    ln_xt[p][:], ln_xt[p][:], R_bc[:]), waits=[t_ld, tc1], inc=SV)
                t_a2 = pb.op("vector", lambda e, p=tp, h=hk: e.tensor_add(
                    dst_slice_fn(h), ln_xt[p][:], NMR_bc[:]),
                    waits=[t_a1, tc2] + list(dst_free), inc=SV)
                applies.append(t_a2)
                out_ticks.append(t_a2)
                ln_state["slot_users"][tp] = t_a2
            ln_state["apply"] = applies[-2:]
            return out_ticks, tc2
        if split:
            return finish
        return finish()

    def emit_ffn(src, dst, final_pass):
        tag = "f2" if final_pass else "f1"
        ar = mk_arena()
        xln = [ar.sbuf_tensor(f"{tag}_xln{i}", [128, c.HK, 512], BF16)
               for i in range(2)]
        fuse = ar.sbuf_tensor(f"{tag}_fuse", [128, c.IK, 512], BF16)
        wgb = [ar.sbuf_tensor(f"{tag}_wg{i}", [128, c.HK, 128], BF16)
               for i in range(4)]
        wub = [ar.sbuf_tensor(f"{tag}_wu{i}", [128, c.HK, 128], BF16)
               for i in range(4)]
        wdb = [ar.sbuf_tensor(f"{tag}_wd{i}", [128, c.IK, 128], BF16)
               for i in range(2)]
        gat = [ar.sbuf_tensor(f"{tag}_gat{i}", [128, 512], F32)
               for i in range(2)]
        rxt = [ar.sbuf_tensor(f"{tag}_rxt{i}", [128, 512], F32)
               for i in range(4)]
        xnew = [ar.sbuf_tensor(f"{tag}_xnw{i}", [128, 512], F32)
               for i in range(2)]
        stg = [ar.sbuf_tensor(f"{tag}_stg{i}", [128, 128], F32)
               for i in range(4)] if final_pass else None
        ar.allocate()
        s_wg = [mkds(f"{tag}_swg{i}", "gpsimd") for i in range(4)]
        s_wu = [mkds(f"{tag}_swu{i}", "gpsimd") for i in range(4)]
        s_wd = [mkds(f"{tag}_swd{i}", "gpsimd") for i in range(2)]
        s_rx = [mkds(f"{tag}_srx{i}", "sync") for i in range(4)]
        s_xst = [mkds(f"{tag}_sxst{i}", "sync") for i in range(2)]
        s_on = [mkds(f"{tag}_son{i}", "sync") for i in range(4)]

        prev_up_mm = {}
        last_blk_mms = []
        t_fuse = {}
        t_sil = {}
        t_dn_mm = {}
        t_dn_ev = {}
        t_xn_st = {}
        t_tr_ev = []
        t_trs = []
        t_onat_st = [None] * 4
        xnw_tr_user = {}
        pend_tr = [None]
        prev_block_dn = []
        rx_user = {}
        pf = {}
        PAIRS = ((0, 2), (1, 3), (4, 5))
        pair_sil = {0: None, 1: None, 2: None}
        pair_fus = {0: None, 1: None, 2: None}
        t_xln_next, _ = emit_ln(src, 0, lambda h: xln[0][:, h, :],
                                dst_free=[])
        for blk in range(c.NTB):
            tok0 = blk * c.TB
            xp = blk % 2
            t_xln = t_xln_next
            gate_w0_extra = t_tr_ev[-4:] if final_pass else []
            prev_up_mm = {k - c.IK: prev_up_mm.get(k)
                          for k in range(c.IK - 4, c.IK)}
            t_sil = {-2: t_sil.get(c.IK - 2), -1: t_sil.get(c.IK - 1)}
            t_fuse = {-2: t_fuse.get(c.IK - 2), -1: t_fuse.get(c.IK - 1)}
            for it in range(c.IK):
                par = it % 2
                sl4 = (blk * c.IK + it) % 4
                if (blk, it) in pf:
                    t_wg1, t_wg2 = pf.pop((blk, it))
                else:
                    w_w = [prev_up_mm.get(it - 4)]
                    t_wg1 = pb.op("gpsimd", lambda e, i=it, p=sl4: e.dma_start(
                        wgb[p][:], di["wg_t"][i]), waits=w_w,
                        inc=s_wg[sl4], k=16)
                    t_wg2 = pb.op("gpsimd", lambda e, i=it, p=sl4: e.dma_start(
                        wub[p][:], di["wu_t"][i]), waits=w_w,
                        inc=s_wu[sl4], k=16)
                pi = (blk * c.IK + it) % 3
                gb, ub = PAIRS[pi]
                tg = tu = None
                wg0 = [t_wg1] + t_xln + gate_w0_extra + [pair_sil[pi]]
                for hk in range(c.HK):
                    tg = pb.op("tensor", lambda e, p=sl4, h=hk, b=gb, x=xp:
                               e.matmul(bank(b), wgb[p][:, h, :], xln[x][:, h, :],
                                        start=(h == 0), stop=(h == c.HK - 1)),
                               waits=wg0 if hk == 0 else (), inc=STn)
                wu0 = [t_wg2] + [pair_fus[pi]]
                for hk in range(c.HK):
                    tu = pb.op("tensor", lambda e, p=sl4, h=hk, b=ub, x=xp:
                               e.matmul(bank(b), wub[p][:, h, :], xln[x][:, h, :],
                                        start=(h == 0), stop=(h == c.HK - 1)),
                               waits=wu0 if hk == 0 else (), inc=STn)
                prev_up_mm[it] = tu
                ts = pb.op("scalar", lambda e, i=it, b=gb, p=par: e.activation(
                    gat[p][:], bank(b), AF.Silu, scale=cst["sg_c"][:, i:i + 1]),
                    waits=[tg, t_sil.get(it - 1), t_fuse.get(it - 2)],
                    inc=SS)
                t_sil[it] = ts
                pair_sil[pi] = ts
                t_fuse[it] = pb.op(
                    "vector", lambda e, i=it, b=ub, p=par: e.scalar_tensor_tensor(
                        fuse[:, i, :], bank(b), cst["su_c"][:, i:i + 1],
                        gat[p][:], ALU.mult, ALU.mult),
                    waits=[tu, ts] + prev_block_dn, inc=SV)
                pair_fus[pi] = t_fuse[it]
                if it == min(24, c.IK - 2) and blk + 1 < c.NTB:
                    ln_fin = emit_ln(
                        src, (blk + 1) * c.TB,
                        lambda h, x=(blk + 1) % 2: xln[x][:, h, :],
                        dst_free=[prev_up_mm.get(it - 2)], split=True)
                if it == min(40, c.IK - 1) and blk + 1 < c.NTB:
                    t_xln_next, _ = ln_fin()
            last_blk_mms = [prev_up_mm[c.IK - 1], prev_up_mm[c.IK - 2]]
            all_fuse = [t_fuse[i] for i in range(c.IK)]
            prev_block_dn = []
            t_dn_mm = {-2: t_dn_mm.get(c.HK - 2), -1: t_dn_mm.get(c.HK - 1)}
            t_dn_ev = {k - c.HK: t_dn_ev.get(c.HK + (k - c.HK))
                       for k in range(c.HK - 4, c.HK)}
            t_rx_all = {}
            for ht in range(min(4, c.HK)):
                t_rx_all[ht] = pb.op(
                    "sync", lambda e, h=ht, p=ht % 4, t0=tok0: e.dma_start(
                        rxt[p][:], src[h, :, t0:t0 + 512]),
                    waits=[rx_user.get(ht % 4)], inc=s_rx[ht % 4], k=16)
            for ht in range(c.HK):
                par = ht % 2
                if ht == min(2, c.HK - 1) and blk + 1 < c.NTB:
                    for nit in range(min(2, c.IK)):
                        sl4 = ((blk + 1) * c.IK + nit) % 4
                        w_w = [prev_up_mm.get(c.IK - 4 + nit)]
                        t1 = pb.op("gpsimd", lambda e, i=nit, p=sl4:
                                   e.dma_start(wgb[p][:], di["wg_t"][i]),
                                   waits=w_w, inc=s_wg[sl4], k=16)
                        t2 = pb.op("gpsimd", lambda e, i=nit, p=sl4:
                                   e.dma_start(wub[p][:], di["wu_t"][i]),
                                   waits=w_w, inc=s_wu[sl4], k=16)
                        pf[(blk + 1, nit)] = (t1, t2)
                t_wd = pb.op("gpsimd", lambda e, h=ht, p=par: e.dma_start(
                    wdb[p][:], di["wd_t"][h]),
                    waits=[t_dn_mm.get(ht - 2)], inc=s_wd[par], k=16)
                t_rx = t_rx_all[ht]
                rp4 = ht % 4
                dmm = None
                w0 = [t_wd] + all_fuse + [t_dn_ev.get(ht - 2)]
                for it in range(c.IK):
                    dmm = pb.op("tensor", lambda e, i=it, h=ht, p=par:
                                e.matmul(bank(4 + p), wdb[p][:, i, :],
                                         fuse[:, i, :], start=(i == 0),
                                         stop=(i == c.IK - 1)),
                                waits=w0 if it == 0 else (), inc=STn)
                t_dn_mm[ht] = dmm
                if final_pass and pend_tr[0] is not None \
                        and pend_tr[0][1] == par:
                    emit_tr(*pend_tr[0])
                    pend_tr[0] = None
                wev = [dmm, t_rx]
                if final_pass:
                    wev += [xnw_tr_user.get(par)]
                else:
                    wev += [t_xn_st.get(par)]
                t_ev = pb.op("vector", lambda e, h=ht, p=par, r4=rp4:
                             e.scalar_tensor_tensor(
                                 xnew[p][:], bank(4 + p), half_sd[:, h:h + 1],
                                 rxt[r4][:], ALU.mult, ALU.add),
                             waits=wev, inc=SV)
                t_dn_ev[ht] = t_ev
                rx_user[rp4] = t_ev
                prev_block_dn.append(t_ev)
                if ht + 4 < c.HK:
                    t_rx_all[ht + 4] = pb.op(
                        "sync", lambda e, h=ht + 4, p=(ht + 4) % 4, t0=tok0:
                        e.dma_start(rxt[p][:], src[h, :, t0:t0 + 512]),
                        waits=[t_ev], inc=s_rx[(ht + 4) % 4], k=16)
                if not final_pass:
                    t_xn_st[par] = pb.op(
                        "sync", lambda e, h=ht, p=par, t0=tok0: e.dma_start(
                            dst[h, :, t0:t0 + 512], xnew[p][:]),
                        waits=[t_ev], inc=s_xst[par], k=16)
                else:
                    def emit_tr(h2, p2, ev2, b2):
                        for tt in range(4):
                            pbk = tt % 2
                            slot = (h2 * 4 + tt) % 4
                            t_tr = pb.op(
                                "tensor", lambda e, p=p2, t=tt, b=pbk:
                                e.transpose(bank(b, 128),
                                            xnew[p][:, t * 128:(t + 1) * 128],
                                            cst["ident_f"][:]),
                                waits=[ev2] +
                                (t_tr_ev[-2:-1] if len(t_tr_ev) >= 2 else []),
                                inc=STn)
                            t_trs.append(t_tr)
                            xnw_tr_user[p2] = t_tr
                            t_oe = pb.op("scalar",
                                         lambda e, sl=slot, b=pbk: e.copy(
                                             stg[sl][:], bank(b, 128)),
                                         waits=[t_tr, t_onat_st[slot]], inc=SS)
                            t_tr_ev.append(t_oe)
                            t_onat_st[slot] = pb.op(
                                "sync", lambda e, sl=slot, h=h2,
                                r=b2 * 4 + tt: e.dma_start(
                                    out_nat[r * 128:(r + 1) * 128,
                                            h * 128:(h + 1) * 128], stg[sl][:]),
                                waits=[t_oe], inc=s_on[slot], k=16)
                    if pend_tr[0] is not None:
                        emit_tr(*pend_tr[0])
                    pend_tr[0] = (ht, par, t_ev, blk)
            le = c.HK - 1 if (c.HK - 1) % 2 == 0 else c.HK - 2
            lo = c.HK - 1 if (c.HK - 1) % 2 == 1 else c.HK - 2
            pair_sil[2] = t_dn_ev[le]
            pair_fus[2] = t_dn_ev[lo] if lo >= 0 else None

        if final_pass and pend_tr[0] is not None:
            h2, p2, ev2, b2 = pend_tr[0]
            for tt in range(4):
                pbk = tt % 2
                slot = (h2 * 4 + tt) % 4
                t_tr = pb.op("tensor", lambda e, p=p2, t=tt, b=pbk:
                             e.transpose(bank(b, 128),
                                         xnew[p][:, t * 128:(t + 1) * 128],
                                         cst["ident_f"][:]),
                             waits=[ev2] + t_tr_ev[-2:-1], inc=STn)
                t_oe = pb.op("scalar", lambda e, sl=slot, b=pbk: e.copy(
                    stg[sl][:], bank(b, 128)),
                    waits=[t_tr, t_onat_st[slot]], inc=SS)
                t_tr_ev.append(t_oe)
                t_onat_st[slot] = pb.op(
                    "sync", lambda e, sl=slot, h=h2, r=b2 * 4 + tt:
                    e.dma_start(out_nat[r * 128:(r + 1) * 128,
                                        h * 128:(h + 1) * 128], stg[sl][:]),
                    waits=[t_oe], inc=s_on[slot], k=16)
        ar.free()

    def emit_attn(src, dst):
        NH, SK, HK, NG = c.NH, c.SK, c.HK, c.NG
        NCH = c.S // 512
        NWQ = 3
        g_rd = {i: [] for i in range(7)}
        g_vps = []
        g_aT = {0: [], 1: []}
        natc = [0]
        stc = [0]
        SB3 = (0, 1)

        ar = mk_arena()
        xln_at2 = [ar.sbuf_tensor(f"at_xln{bb}", [128, HK, c.S], BF16)
                   for bb in range(c.BPC)]
        wqb = [ar.sbuf_tensor(f"at_wq{i}", [128, HK, 128], BF16)
               for i in range(NWQ)]
        qst = [ar.sbuf_tensor(f"at_qst{i}", [128, 512], BF16)
               for i in range(4)]
        ar.allocate()
        s_wq = [mkds(f"s_wq{i}", "gpsimd") for i in range(NWQ)]
        s_qst = [mkds(f"s_qst{i}", "sync") for i in range(4)]

        xln_tick = {}
        ready = set()
        ln_count = [0]

        def emit_attn_ln(bb, cc, split=False):
            lb = 5 + ln_count[0] % 2
            ln_count[0] += 1
            r = emit_ln(src, bb * c.S + cc * 512,
                        lambda h, x=bb, c0=cc:
                        xln_at2[x][:, h, c0 * 512:(c0 + 1) * 512],
                        lbank=lb, split=split)
            if split:
                def fin(r=r, bb=bb, cc=cc, lb=lb):
                    tk, tc2 = r()
                    xln_tick[(bb, cc)] = tk
                    g_rd[lb].append(tc2)
                return fin
            tk, tc2 = r
            xln_tick[(bb, cc)] = tk
            g_rd[lb].append(tc2)

        qkc = [0]
        t_qst = {}
        qkv_done = {}
        ring_last = {}

        def emit_slab(h, p, pairs, ws):
            fk = p * NH + h
            t_w = pb.op("gpsimd", lambda e, f=fk, w=ws: e.dma_start(
                wqb[w][:], di["wqkv_t"][f]),
                waits=[ring_last.get(ws)], inc=s_wq[ws], k=16)
            mm = None
            for (bb, cc) in pairs:
                bq = qkc[0] % 5
                qkc[0] += 1
                qs = qkc[0] % 4
                tk = xln_tick[(bb, cc)]
                for hk in range(HK):
                    w0 = []
                    if hk == 0:
                        w0 = [t_w] + g_rd[bq][-1:]
                    w0 = w0 + [tk[hk]]
                    mm = pb.op("tensor",
                               lambda e, w=ws, k=hk, b=bq, c0=cc, x=bb:
                               e.matmul(bank(b), wqb[w][:, k, :],
                                        xln_at2[x][:, k,
                                                   c0 * 512:(c0 + 1) * 512],
                                        start=(k == 0),
                                        stop=(k == HK - 1)),
                               waits=w0, inc=STn)
                tev = pb.op("scalar",
                            lambda e, f=fk, b=bq, q=qs:
                            e.activation(
                                qst[q][:], bank(b), AF.Identity,
                                bias=cst["qkvb_c"][:, f:f + 1]),
                            waits=[mm, t_qst.get(qs)], inc=SS)
                g_rd[bq].append(tev)
                t_qst[qs] = pb.op(
                    "sync", lambda e, hh=bb * NH + h, pp=p, c0=cc, q=qs:
                    e.dma_start(
                        qkvT_d[hh, pp][:, c0 * 512:(c0 + 1) * 512],
                        qst[q][:]), waits=[tev], inc=s_qst[qs], k=16)
            qkv_done[(h, p)] = mm
            ring_last[ws] = mm

        pend = {}
        allc = [(bb, cc) for bb in range(c.BPC) for cc in range(NCH)]
        n_up = min(2, NCH)
        for cc in range(n_up):
            emit_attn_ln(0, cc)
            ready.add((0, cc))
        rem = allc[n_up:]
        stats_sched = {2 + 2 * k: rem[k] for k in range(len(rem))}
        fin_sched = {3 + 2 * k: rem[k] for k in range(len(rem))}
        deferred = []
        si = 0
        for h in range(NH):
            for p in range(3):
                if si in stats_sched:
                    bb, cc = stats_sched[si]
                    pend[(bb, cc)] = emit_attn_ln(bb, cc, split=True)
                if si in fin_sched:
                    bb, cc = fin_sched[si]
                    pend.pop((bb, cc))()
                    ready.add((bb, cc))
                pairs = [pr for pr in allc if pr in ready]
                miss = [pr for pr in allc if pr not in ready]
                emit_slab(h, p, pairs, si % NWQ)
                if miss:
                    deferred.append((h, p, miss))
                si += 1
        for pr in rem:
            if pr in pend:
                pend.pop(pr)()
                ready.add(pr)
            elif pr not in ready:
                emit_attn_ln(*pr)
                ready.add(pr)
        for (h, p, prs) in deferred:
            emit_slab(h, p, prs, si % NWQ)
            si += 1
        dense_tail = [t_qst.get(i) for i in range(4)]
        barrier()
        ar.free()

        ar = mk_arena()
        qT = [ar.sbuf_tensor(f"at_q{i}", [128, c.S], BF16) for i in range(2)]
        kT = [ar.sbuf_tensor(f"at_k{i}", [128, c.S], BF16) for i in range(2)]
        vT = [ar.sbuf_tensor("at_v0", [128, c.S], BF16)] * 2
        Vb = [ar.sbuf_tensor(f"at_V{i}", [128, SK, 128], BF16)
              for i in range(2)]
        attnT = [ar.sbuf_tensor("at_aT0", [128, SK, 512], BF16)] * 2
        ctxT_h = ar.sbuf_tensor("at_ctx", [128, c.S], BF16)
        m_all = ar.sbuf_tensor("at_mall", [128, 2 * SK], F32)
        nb_bf = ar.sbuf_tensor("at_nbbf", [128, SK], BF16)
        nb_f = ar.sbuf_tensor("at_nbf", [128, SK], F32)
        cmx = ar.sbuf_tensor("at_cmx", [128, 4], F32)
        lps = ar.sbuf_tensor("at_lps", [128, 4], F32)
        dscr = ar.sbuf_tensor("at_dscr", [128, 512], BF16)
        nmT = ar.sbuf_tensor("at_nmT", [2 * SK, 128], BF16)
        nrow2 = ar.sbuf_tensor("at_nr2", [1, 2 * c.S], BF16)
        rl_bc = ar.sbuf_tensor("at_rlbc", [128, 512], F32)
        ctx_q = [ar.sbuf_tensor(f"at_cq{i}", [128, 512], BF16) for i in range(3)]
        wos16 = [ar.sbuf_tensor(f"at_wo{i}", [128, NH, 128], BF16)
                 for i in range(HK)]
        rxo = [ar.sbuf_tensor(f"at_rxo{i}", [128, 512], F32)
               for i in range(min(7, HK))]
        x2w = [ar.sbuf_tensor(f"at_x2w{i}", [128, 512], F32) for i in range(4)]
        arh = [ar.sbuf_tensor(f"at_arh{i}", [1, c.S], BF16) for i in range(2)]
        arl2 = [ar.sbuf_tensor(f"at_arl2{i}", [1, c.S], BF16)
                for i in range(2)]
        ar.allocate()
        s_wos = mkds("s_wos", "gpsimd")
        s_cq = [mkds(f"s_cq{i}", "sync") for i in range(3)]
        s_rxo = [mkds(f"s_rxo{i}", "sync") for i in range(min(7, HK))]
        s_x2 = [mkds(f"s_x2{i}", "sync") for i in range(4)]
        s_ctxst = mkds("s_ctxst", "sync")
        s_b1 = mkds("s_b1", "sync")
        s_b2 = mkds("s_b2", "sync")
        s_ar = [mkds(f"s_ar{i}", "sync") for i in range(4)]
        s_ql = [mkds(f"s_ql{i}", "sync") for i in range(2)]
        s_kl = [mkds(f"s_kl{i}", "sync") for i in range(2)]
        s_vl = mkds("s_vl", "sync")

        wos_rdy = {0: None}
        for b in range(c.BPC):
            s0 = b * c.S
            H = [dict() for _ in range(NH)]

            def load_gen(h):
                st = H[h]
                par = h % 2
                pw = H[h - 1] if h > 0 else {}
                pw2 = H[h - 2] if h > 1 else {}
                qkv_tail = dense_tail
                st["ql"] = pb.op("sync", lambda e, hh=b * NH + h, p=par:
                    e.dma_start(qT[p][:], qkvT_d[hh, 0]),
                    waits=qkv_tail + [pw2.get("last_sT_mm"),
                                      pw2.get("last_nat_mm")],
                    inc=s_ql[par], k=16)
                st["kl"] = pb.op("sync", lambda e, hh=b * NH + h, p=par:
                    e.dma_start(kT[p][:], qkvT_d[hh, 1]),
                    waits=qkv_tail + [pw2.get("last_sT_mm"),
                                      pw2.get("last_nat_mm")],
                    inc=s_kl[par], k=16)
                st["vl"] = pb.op("sync", lambda e, hh=b * NH + h:
                    e.dma_start(vT[0][:], qkvT_d[hh, 2]),
                    waits=qkv_tail + [pw.get("last_vtr")], inc=s_vl, k=16)
                st["ar_ld"] = pb.op(
                    "sync", lambda e, hh=h, p=par: e.dma_start(
                        arh[p][:], di["arow_hi"][hh:hh + 1, :]),
                    waits=[pw2.get("last_nat_mm")], inc=s_ar[2 * par], k=16)
                st["ar_ld2"] = pb.op(
                    "sync", lambda e, hh=h, p=par: e.dma_start(
                        arl2[p][:], di["arow_lo"][hh:hh + 1, :]),
                    waits=[pw2.get("last_nat_mm")],
                    inc=s_ar[2 * par + 1], k=16)
                st["q"] = [st["ql"]]
                st["k"] = [st["kl"]]
                if False:
                    yield

            def vtrans_gen(h):
                st = H[h]
                par = h % 2
                st["Ve"] = evs = []
                last_tr = None
                for jt in range(SK):
                    slot = jt % 4
                    t_tr = pb.op("tensor", lambda e, j=jt, s=slot:
                                 e.transpose(ps_bf[:, s * 128:(s + 1) * 128],
                                             vT[0][:, j * 128:(j + 1) * 128],
                                             cst["ident_bf"][:]),
                                 waits=[st["vl"]] + (evs[-1:] if evs else
                                                     g_vps[-1:]),
                                 inc=STn)
                    t_ev = pb.op("scalar", lambda e, j=jt, p=par, s=slot: e.copy(
                        Vb[p][:, j, :], ps_bf[:, s * 128:(s + 1) * 128]),
                        waits=[t_tr], inc=SS)
                    g_vps.append(t_ev)
                    evs.append(t_ev)
                    last_tr = t_tr
                    st["last_vtr"] = last_tr
                    yield

            def natural_gen(h):
                st = H[h]
                par = h % 2
                pw = H[h - 1] if h > 0 else {}
                t_red_all = []
                t_l_all = []
                last_mm = None
                for it in range(SK):
                    jext = (it + 1) * 128
                    nch = (jext + 511) // 512
                    t_its = []
                    banks_used = []
                    for ch in range(nch):
                        j0 = ch * 512
                        w = min(512, jext - j0)
                        bq = 2 + natc[0] % 4
                        natc[0] += 1
                        banks_used.append((bq, w))
                        diag = (j0 + w == jext)
                        pb.op("tensor", lambda e, i=it, j=j0, ww=w, b=bq,
                              p=par: e.matmul(
                                  bank(b, ww),
                                  qT[p][:, i * 128:(i + 1) * 128],
                                  kT[p][:, j:j + ww], start=True, stop=False),
                              waits=st["q"] + st["k"] + g_rd[bq][-1:] +
                              [st["ar_ld"], st["ar_ld2"]], inc=STn)
                        pb.op("tensor", lambda e, j=j0, ww=w, b=bq, p=par:
                              e.matmul(bank(b, ww), cst["ones_row_bf"][:],
                                       arh[p][0:1, j:j + ww],
                                       start=False, stop=False), inc=STn)
                        mm = pb.op("tensor", lambda e, j=j0, ww=w, b=bq, p=par:
                                   e.matmul(bank(b, ww), cst["ones_row_bf"][:],
                                            arl2[p][0:1, j:j + ww],
                                            start=False, stop=True), inc=STn)
                        last_mm = mm
                        wred = [mm] + (t_red_all[-1:] if ch == 0 else [])
                        if diag:
                            tdm = pb.op("vector", lambda e, b=bq, ww=w:
                                        e.tensor_add(
                                            bank(b, ww)[:, ww - 128:ww],
                                            bank(b, ww)[:, ww - 128:ww],
                                            cst["tri_n"][:]),
                                        waits=[mm], inc=SV)
                            wred.append(tdm)
                        tr = pb.op("vector", lambda e, b=bq, ww=w, ch_=ch:
                                   e.reduce_max(cmx[:, ch_:ch_ + 1],
                                                bank(b, ww), AX.X),
                                   waits=wred, inc=SV)
                        t_its.append(tr)
                    t_comb = pb.op(
                        "vector", lambda e, i=it, n=nch: e.reduce_max(
                            m_all[:, i:i + 1], cmx[:, 0:n], AX.X,
                            negate=True),
                        waits=t_its + [pw.get("m_tr")], inc=SV)
                    t_red_all.append(t_comb)
                    tb1 = pb.op("vector", lambda e, i=it: e.tensor_copy(
                        nb_bf[:, i:i + 1], m_all[:, i:i + 1]),
                        waits=[t_comb], inc=SV)
                    tb2 = pb.op("vector", lambda e, i=it: e.tensor_copy(
                        nb_f[:, i:i + 1], nb_bf[:, i:i + 1]),
                        waits=[tb1], inc=SV)
                    t_exps = []
                    for ch in range(nch):
                        bq, w = banks_used[ch]
                        te = pb.op("scalar", lambda e, b=bq, ww=w, i=it, c_=ch:
                                   e.activation(
                                       dscr[:, 0:ww], bank(b, ww), AF.Exp,
                                       bias=nb_f[:, i:i + 1],
                                       accum_out=lps[:, c_:c_ + 1]),
                                   waits=[tb2] +
                                   (t_exps[-1:] if ch else t_l_all[-1:]),
                                   inc=SS)
                        g_rd[bq].append(te)
                        t_exps.append(te)
                    t_l_all.append(pb.op(
                        "vector", lambda e, i=it, n=nch: e.reduce_sum(
                            m_all[:, SK + i:SK + i + 1], lps[:, 0:n], AX.X),
                        waits=t_exps, inc=SV))
                    st["last_nat_mm"] = last_mm
                    yield
                st["t_l_all"] = t_l_all

            def natural_tail(h):
                st = H[h]
                pw = H[h - 1] if h > 0 else {}
                t_rc = pb.op("vector", lambda e: e.reciprocal(
                    m_all[:, SK:2 * SK], m_all[:, SK:2 * SK]),
                    waits=st["t_l_all"], inc=SV)
                t_tr = pb.op("tensor", lambda e: e.transpose(
                    ps[0:2 * SK, 1024:1152], m_all[:], cst["ident_f"][:]),
                    waits=[t_rc] + g_rd[2][-1:], inc=STn)
                st["m_tr"] = t_tr
                t_ev = pb.op("scalar", lambda e: e.copy(
                    nmT[:], ps[0:2 * SK, 1024:1152]),
                    waits=[t_tr, pw.get("b1")], inc=SS)
                g_rd[2].append(t_ev)
                t_b1 = pb.op("sync", lambda e: e.dma_start(
                    bounce_d.reshape([2 * SK, 128])[:], nmT[:]),
                    waits=[t_ev, pw.get("b2")], inc=s_b1, k=16)
                st["b1"] = t_b1
                t_b2 = pb.op("sync", lambda e: e.dma_start(
                    nrow2[:], bounce_d[:]),
                    waits=[t_b1, pw.get("last_rank1"), pw.get("last_rlb")],
                    inc=s_b2, k=16)
                st["b2"] = t_b2
                st["cast"] = t_b2

            def groups_gen(h):
                st = H[h]
                par = h % 2
                pw = H[h - 1] if h > 0 else {}
                t_ctx_done = []
                for g in range(NG):
                    gpar = g % 2
                    t_exps = []
                    aT_w = g_aT[0][-1:]
                    for jt in range(4 * g + 4):
                        sb_ = SB3[stc[0] % 2]
                        stc[0] += 1
                        lo = max(0, (jt - 4 * g)) * 128
                        mm = pb.op("tensor", lambda e, j=jt, b=sb_, p=par, g_=g:
                                   e.matmul(bank(b),
                                            kT[p][:, j * 128:(j + 1) * 128],
                                            qT[p][:, g_ * 512:(g_ + 1) * 512],
                                            start=True, stop=False),
                                   waits=[st.get("cast")] + st["q"] + st["k"] +
                                   g_rd[sb_][-1:], inc=STn)
                        st["last_sT_mm"] = mm
                        mm2 = pb.op("tensor", lambda e, b=sb_, g_=g: e.matmul(
                            bank(b), cst["ones_row_bf"][:],
                            nrow2[0:1, g_ * 512:(g_ + 1) * 512],
                            start=False, stop=True), inc=STn)
                        st["last_rank1"] = mm2
                        tw = [mm2]
                        if jt >= 4 * g:
                            dt_ = pb.op("vector",
                                        lambda e, b=sb_, l=lo: e.tensor_add(
                                            bank(b)[:, l:l + 128],
                                            bank(b)[:, l:l + 128],
                                            cst["tri_t"][:]),
                                        waits=[mm2], inc=SV)
                            tw = [dt_]
                        if lo > 0:
                            tms = pb.op("vector", lambda e, j=jt, l=lo:
                                        e.memset(attnT[0][:, j, 0:l], 0),
                                        waits=aT_w, inc=SV)
                            tw.append(tms)
                        t_e = pb.op("scalar", lambda e, j=jt, b=sb_, l=lo,
                                    hh=h: e.activation(
                                        attnT[0][:, j, l:512],
                                        bank(b)[:, l:512], AF.Exp,
                                        bias=cst["alibi_c"][:,
                                                            hh * SK + j:
                                                            hh * SK + j + 1]),
                                    waits=tw + aT_w, inc=SS)
                        g_rd[sb_].append(t_e)
                        t_exps.append(t_e)
                        yield
                    bb = SB3[stc[0] % 2]
                    stc[0] += 1
                    t_rb = pb.op("tensor", lambda e, b=bb, g_=g: e.matmul(
                        bank(b), cst["ones_row_bf"][:],
                        nrow2[0:1, c.S + g_ * 512:c.S + (g_ + 1) * 512],
                        start=True, stop=True),
                        waits=[st.get("cast")] + g_rd[bb][-1:], inc=STn)
                    st["last_rlb"] = t_rb
                    t_rc = pb.op("vector", lambda e, b=bb: e.tensor_copy(
                        rl_bc[:], bank(b)), waits=[t_rb] + t_ctx_done[-1:],
                        inc=SV)
                    g_rd[bb].append(t_rc)
                    t_cmm = None
                    for jt in range(4 * g + 4):
                        t_cmm = pb.op("tensor", lambda e, j=jt, p=par,
                                      f=(jt == 0), la=(jt == 4 * g + 3):
                                      e.matmul(bank(6), Vb[p][:, j, :],
                                               attnT[0][:, j, :],
                                               start=f, stop=la),
                                      waits=(t_exps + g_rd[6][-1:] +
                                             H[h]["Ve"])
                                      if jt == 0 else (), inc=STn)
                        g_aT[0].append(t_cmm)
                    st["last_ctx_mm"] = t_cmm
                    t_cev = pb.op("vector", lambda e, g_=g: e.tensor_mul(
                        ctxT_h[:, g_ * 512:(g_ + 1) * 512], bank(6), rl_bc[:]),
                        waits=[t_cmm, t_rc, pw.get("ctx_store")], inc=SV)
                    g_rd[6].append(t_cev)
                    t_ctx_done.append(t_cev)
                    yield
                st["ctx_store"] = pb.op("sync", lambda e, hh=h: e.dma_start(
                    ctxT_d[hh], ctxT_h[:]), waits=t_ctx_done, inc=s_ctxst, k=16)

            from itertools import chain as _chain

            def drive(a_gens, b_gens):
                ita = _chain(*a_gens)
                itb = _chain(*b_gens)
                alive_a = alive_b = True
                turn_a = True
                while alive_a or alive_b:
                    it = ita if (turn_a and alive_a) or not alive_b else itb
                    try:
                        next(it)
                    except StopIteration:
                        if it is ita:
                            alive_a = False
                        else:
                            alive_b = False
                    turn_a = not turn_a

            SKIP = os.environ.get("K_SKIP", "")
            list(load_gen(0))
            if "nat" not in SKIP:
                list(natural_gen(0))
                natural_tail(0)
            list(vtrans_gen(0))
            for h in range(NH):
                if h + 1 < NH:
                    list(load_gen(h + 1))
                    drive([groups_gen(h)] if "grp" not in SKIP else [],
                          [natural_gen(h + 1)] if "nat" not in SKIP else [])
                    if "nat" not in SKIP:
                        natural_tail(h + 1)
                    list(vtrans_gen(h + 1))
                else:
                    if "grp" not in SKIP:
                        drive([groups_gen(h)], [])
            ctx_stores = [H[h].get("ctx_store") for h in range(NH)]
            barrier()
            if b == 0:
                t_wosl = None
                for k in range(HK):
                    t_wosl = pb.op("gpsimd", lambda e, k_=k: e.dma_start(
                        wos16[k_][:], di["wout_t"][k_]), inc=s_wos, k=16)
                wos_rdy[0] = t_wosl
            QW = min(7, HK)
            prev_ev = {}
            prev_rx = {}
            prev_st = {}
            prev_cx = {}
            xw = [0]
            for q0 in range(0, HK, QW):
                hks = list(range(q0, min(q0 + QW, HK)))
                nb = len(hks)
                seq = [(ch, fk) for ch in range(NCH) for fk in range(NH)]
                t_cx_all = {}
                for si in range(min(3, len(seq))):
                    ch, fk = seq[si]
                    t_cx_all[si] = pb.op(
                        "sync", lambda e, f=fk, c0=ch, p=si % 3:
                        e.dma_start(ctx_q[p][:],
                                    ctxT_d[f][:, c0 * 512:(c0 + 1) * 512]),
                        waits=prev_cx.get(si % 3, []), inc=s_cq[si % 3], k=16)
                mmq = {}
                last_mm = None
                for si, (ch, fk) in enumerate(seq):
                    cpar = si % 3
                    t_cx = t_cx_all[si]
                    if fk == 0:
                        for j, hk in enumerate(hks):
                            prev_rx[j] = pb.op(
                                "sync", lambda e, k=hk, c0=ch, j_=j, s_=s0:
                                e.dma_start(rxo[j_][:],
                                            src[k, :, s_ + c0 * 512:
                                                s_ + (c0 + 1) * 512]),
                                waits=[prev_ev.get(j)], inc=s_rxo[j], k=16)
                    for j, hk in enumerate(hks):
                        w0 = [t_cx]
                        if fk == 0:
                            w0 = w0 + [wos_rdy[0], prev_ev.get(j)]
                        mmq[j] = pb.op(
                            "tensor", lambda e, k=hk, f=fk, p=cpar, b=j:
                            e.matmul(bank(b), wos16[k][:, f, :], ctx_q[p][:],
                                     start=(f == 0), stop=(f == NH - 1)),
                            waits=w0, inc=STn)
                    last_mm = mmq[nb - 1]
                    if si + 3 < len(seq):
                        ch2, fk2 = seq[si + 3]
                        t_cx_all[si + 3] = pb.op(
                            "sync", lambda e, f=fk2, c0=ch2, p=(si + 3) % 3:
                            e.dma_start(ctx_q[p][:],
                                        ctxT_d[f][:, c0 * 512:(c0 + 1) * 512]),
                            waits=[last_mm], inc=s_cq[(si + 3) % 3], k=16)
                    if fk == NH - 1:
                        for j, hk in enumerate(hks):
                            xp2 = xw[0] % 4
                            xw[0] += 1
                            t_ev = pb.op(
                                "vector", lambda e, k=hk, b=j, j_=j, p=xp2:
                                e.scalar_tensor_tensor(
                                    x2w[p][:], bank(b),
                                    cst["outb_c"][:, k:k + 1], rxo[j_][:],
                                    ALU.add, ALU.add),
                                waits=[mmq[j], prev_rx.get(j),
                                       prev_st.get(xp2)], inc=SV)
                            prev_ev[j] = t_ev
                            prev_st[xp2] = pb.op(
                                "sync", lambda e, k=hk, c0=ch, p=xp2, s_=s0:
                                e.dma_start(dst[k, :, s_ + c0 * 512:
                                                s_ + (c0 + 1) * 512],
                                            x2w[p][:]), waits=[t_ev],
                                inc=s_x2[xp2], k=16)
                prev_cx = {i: [last_mm] for i in range(3)}
            barrier()
        ar.free()

    emit_ffn(xT_a, xT_b, final_pass=False)
    barrier()
    emit_attn(xT_b, xT_a)
    barrier()
    emit_ffn(xT_a, None, final_pass=True)
    barrier()
    pb.op("sync", lambda e: e.nop())

    with nc.Block() as block:
        @block.tensor
        def _(e):
            for f in pb.q["tensor"]:
                f(e)

        @block.scalar
        def _(e):
            for f in pb.q["scalar"]:
                f(e)

        @block.vector
        def _(e):
            for f in pb.q["vector"]:
                f(e)

        @block.gpsimd
        def _(e):
            for f in pb.q["gpsimd"]:
                f(e)

        @block.sync
        def _(e):
            for f in pb.q["sync"]:
                f(e)

    return nc


def kernel(**inputs):
    cfg = Cfg()
    g, xs = host_prep(cfg, inputs)
    nc = build_program(cfg)
    in_maps = []
    for cid in range(cfg.n_cores):
        m = dict(g)
        m["x_nat"] = xs[cid]
        in_maps.append(m)
    res = run_bass_kernel_spmd(nc, in_maps, core_ids=list(range(cfg.n_cores)))
    outs = [np.asarray(res.results[cid]["out_nat"], dtype=np.float32)
            .reshape(cfg.BPC, cfg.S, cfg.H) for cid in range(cfg.n_cores)]
    return np.concatenate(outs, axis=0)
